# version 01 (final)
# baseline (speedup 1.0000x reference)
import contextlib
import numpy as np
import concourse.bass as bass
import concourse.mybir as mybir
from concourse.bass_utils import run_bass_kernel_spmd

F32 = mybir.dt.float32
BF16 = mybir.dt.bfloat16
AF = mybir.ActivationFunctionType
ALU = mybir.AluOpType
AX = mybir.AxisListType.X

NCORE = 8
TC = 1024
H = 2048
I0 = 8192
DI = 4096
DS = DI // NCORE
NST = 16
TR = 128
I1 = 4096
MC1 = float(3.0 * 2.0 ** 22)
MC2 = float(3.0 * 2.0 ** 51)
EPS = 1e-5
RSQH = float(1.0 / np.sqrt(np.float32(2048.0)))

BIGW = {
    "qkv":  (6144, H),
    "out0": (H, H),
    "gate": (I0, H),
    "up":   (I0, H),
    "down": (H, I0),
    "inp":  (2 * DI, H),
    "outp": (DI, H),
    "l1up": (I1, H),
    "l1dn": (H, I1),
}
QUANTW = ("gate", "up", "down", "l1up", "l1dn")
QPOS = {w: i for i, w in enumerate(QUANTW)}

SMALLW = {
    "b_qkvT": (128, 48), "b_out": (H,),
    "xpw_slT": (DS, 160), "dtw_slT": (TR, DS),
    "dtb_slT": (128, 4), "dv_slT": (128, 4),
    "cw_slT": (128, 16), "cb_slT": (128, 4),
    "alog_slE": (DS, 2048),
}
CONSTS = {
    "c_ident": (128, 128), "c_had128": (128, 128), "c_bd64": (128, 128),
    "c_bd32": (128, 128), "c_bd16": (128, 128), "c_ones": (128, 128),
    "c_repl16": (16, 128), "c_repl8": (8, 128), "c_nred": (128, 8),
    "c_h64": (64, 64), "c_h32": (32, 32), "c_h16": (16, 16),
    "c_dmask2": (128, 2048), "c_nredf": (128, 2048),
}


def _had(n):
    Hm = np.array([[1.0]], dtype=np.float32)
    while Hm.shape[0] < n:
        Hm = np.block([[Hm, Hm], [Hm, -Hm]])
    return Hm.astype(np.float32)


def host_constants():
    import ml_dtypes
    bf = ml_dtypes.bfloat16
    c = {}
    c["c_ident"] = np.eye(128, dtype=np.float32).astype(bf)
    c["c_had128"] = _had(128).astype(bf)
    for nb, nm in ((64, "c_bd64"), (32, "c_bd32"), (16, "c_bd16")):
        Hb = _had(nb)
        blk = np.zeros((128, 128), dtype=np.float32)
        for g in range(128 // nb):
            blk[g * nb:(g + 1) * nb, g * nb:(g + 1) * nb] = Hb
        c[nm] = blk.astype(bf)
    c["c_ones"] = np.ones((128, 128), dtype=np.float32).astype(bf)
    r16 = np.zeros((16, 128), dtype=np.float32)
    for d in range(8):
        r16[:, d * 16:(d + 1) * 16] = np.eye(16)
    c["c_repl16"] = r16.astype(bf)
    r8 = np.zeros((8, 128), dtype=np.float32)
    for d in range(8):
        r8[d, d * 16:(d + 1) * 16] = 1.0
    c["c_repl8"] = r8.astype(bf)
    nr = np.zeros((128, 8), dtype=np.float32)
    for d in range(8):
        nr[d * 16:(d + 1) * 16, d] = 1.0
    c["c_nred"] = nr.astype(bf)
    c["c_h64"] = _had(64).astype(bf)
    c["c_h32"] = _had(32).astype(bf)
    c["c_h16"] = _had(16).astype(bf)
    dm2 = np.zeros((128, 2048), dtype=np.float32)
    for jj in range(16):
        for dd in range(8):
            dm2[8 * jj + dd, jj * 128 + dd * 16:jj * 128 + (dd + 1) * 16] = 1.0
    c["c_dmask2"] = dm2.astype(bf)
    nrf = np.zeros((128, 2048), dtype=np.float32)
    for jj in range(16):
        for dd in range(8):
            nrf[dd * 16:(dd + 1) * 16, jj * 128 + 8 * jj + dd] = 1.0
    c["c_nredf"] = nrf.astype(bf)
    qn = np.zeros((1, 8), dtype=np.float32)
    for i, w in enumerate(QUANTW):
        qn[0, i] = 1.0 / float(BIGW[w][0] * BIGW[w][1])
    c["c_qnumel"] = np.repeat(qn, 128, axis=0)
    c["c_onesf"] = np.ones((128, 128), dtype=np.float32)
    c["c_one"] = np.ones((128, 1), dtype=np.float32)
    return c


class Sem:
    __slots__ = ("h", "n")
    def __init__(self, h):
        self.h = h
        self.n = 0
    def inc(self, instr, k=1):
        instr.then_inc(self.h, k)
        self.n += k
        return self.n
    def dinc(self, instr):
        return self.inc(instr, 16)


class SemView:
    __slots__ = ("s", "b")
    def __init__(self, s):
        self.s = s
        self.b = s.n
    @property
    def h(self):
        return self.s.h
    @property
    def n(self):
        return self.s.n
    def inc(self, instr, k=1):
        return self.s.inc(instr, k)
    def dinc(self, instr):
        return self.s.inc(instr, 16)


class MK:
    def __init__(self, upto=99, dbg=()):
        self.upto = upto
        self.dbg = set(dbg)
        self.st = contextlib.ExitStack()
        self.nc = bass.Bass("TRN2", target_bir_lowering=False)
        self.sems = {}
        self.D = {}

    def sem(self, name):
        if name not in self.sems:
            self.sems[name] = Sem(self.st.enter_context(self.nc.semaphore(name)))
        return self.sems[name]

    def semv(self, role):
        return SemView(self.sem("P_" + role))

    def sb(self, name, shape, dt):
        return self.st.enter_context(self.nc.sbuf_tensor(name, list(shape), dt))

    def psu(self, name, shape, dt=F32):
        return self.st.enter_context(self.nc.psum_tensor(name, list(shape), dt))

    def din(self, name, shape, dt=F32):
        self.D[name] = self.nc.dram_tensor(name, list(shape), dt, kind="ExternalInput")
        return self.D[name]

    def dout(self, name, shape, dt=F32):
        self.D[name] = self.nc.dram_tensor(name, list(shape), dt, kind="ExternalOutput")
        return self.D[name]

    def dint(self, name, shape, dt=F32):
        self.D[name] = self.nc.dram_tensor(name, list(shape), dt)
        return self.D[name]

    def ap(self, name):
        return self.D[name].ap()

    def snapshot(self):
        return [(s, s.n) for s in self.sems.values() if s.n > 0]

    def emit_fence(self, e, fence):
        for s, c in fence:
            e.wait_ge(s.h, c)

    def declare(self):
        self.din("x_in", (TC, H))
        for w, (r, c) in BIGW.items():
            self.din("wsl_" + w, (r // NCORE, c))
        for nm, shape in SMALLW.items():
            self.din(nm, shape)
        for nm, shape in CONSTS.items():
            self.din(nm, shape, BF16)
        self.din("c_qnumel", (128, 8), F32)
        self.din("c_one", (128, 1), F32)
        self.din("c_onesf", (128, 128), F32)
        self.dout("y_out", (TC, H))

        self.dint("xres_a", (TC, H))
        self.dint("xres_b", (TC, H))
        for w, (r, c) in BIGW.items():
            self.dint("tsl_" + w, (c, r // NCORE), BF16)
            self.dint("tw_" + w, (NCORE * c, r // NCORE), BF16)
        if "prep" in self.dbg:
            for w in ("gate", "qkv"):
                r, c = BIGW[w]
                self.dout("dbg_tw_" + w, (NCORE * c, r // NCORE), BF16)
        if "prep" in self.dbg:
            self.dout("dbg_wsc", (128, 8))
            self.dout("dbg_vec", (128, 192))
        self.dint("ws_part", (128, 8))
        self.dint("ws_all", (128, 8))
        self.dint("kag_in", (H, TC))
        self.dint("kag_out", (NCORE * H, TC))
        self.dint("vag_in", (TC, H))
        self.dint("vag_out", (NCORE * TC, H))
        self.dint("kgb", (NCORE * H, TC), BF16)
        self.dint("vgb", (NCORE * TC, H), BF16)
        self.dint("xag_in", (TC, H))
        self.dint("xag_out", (NCORE * TC, H))
        self.dint("xgb", (NCORE * TC, H), BF16)
        self.dint("ymd", (DS, I0), BF16)
        self.dint("yma", (NCORE * TC, DS), BF16)
        self.dint("ymag", (NCORE * NCORE * TC, DS), BF16)
        self.dint("qdl", (TC, DI), BF16)
        self.dint("ssm_in", (8, 160, TC))
        self.dint("ssm_out", (8, 160, TC))
        self.dint("gate_sil", (DS, I0), BF16)
        self.dint("int0", (I0, TC), BF16)
        self.dint("u0", (TC, I0), BF16)
        self.dint("v0", (TC, I0), BF16)
        self.dint("qd0", (TC, I0), BF16)
        self.dint("int1", (I1, TC), BF16)
        self.dint("u1", (TC, I1), BF16)
        self.dint("v1", (TC, I1), BF16)
        self.dint("qd1", (TC, I1), BF16)
        self.dint("uh", (TC, H), BF16)
        self.dint("vh", (TC, H), BF16)
        self.dint("qdh", (TC, H), BF16)
        self.dint("qx0", (TC, H), BF16)
        for nm, shape in (("int0", (I0, TC)), ("u0", (TC, I0)), ("v0", (TC, I0)),
                          ("int1", (I1, TC)), ("u1", (TC, I1)), ("v1", (TC, I1)),
                          ("uh", (TC, H)), ("vh", (TC, H)), ("qx0", (TC, H))):
            self.dint(nm + "_lo", shape, BF16)
        self.dint("vag_inF", (H, TC))
        self.dint("vag_out2", (NCORE * H, TC))
        self.dint("vgbF", (NCORE * H, TC), BF16)
        self.dint("opart", (NCORE * TC, H))
        self.dint("ors", (TC, H))

    def alloc_sbuf(self):
        self.SLAB = self.sb("SLAB", (128, 65536), BF16)
        self.SCR4 = self.sb("SCR4", (128, 8192), F32)
        self.SCR2 = self.sb("SCR2", (128, 8192), BF16)
        self.WST = self.sb("WST", (128, 8192), BF16)
        self.VEC = self.sb("VEC", (128, 768), F32)
        self.IDENT = self.sb("IDENT", (128, 128), BF16)
        self.HAD128 = self.sb("HAD128", (128, 128), BF16)
        self.BD64 = self.sb("BD64", (128, 128), BF16)
        self.BD32 = self.sb("BD32", (128, 128), BF16)
        self.BD16 = self.sb("BD16", (128, 128), BF16)
        self.ONESB = self.sb("ONESB", (128, 128), BF16)
        self.REPL16 = self.sb("REPL16", (16, 128), BF16)
        self.REPL8 = self.sb("REPL8", (8, 128), BF16)
        self.NRED = self.sb("NRED", (128, 8), BF16)
        self.H64S = self.sb("H64S", (64, 64), BF16)
        self.H32S = self.sb("H32S", (32, 32), BF16)
        self.H16S = self.sb("H16S", (16, 16), BF16)
        self.CM = self.sb("CM", (128, 1), F32)
        self.CONE = self.sb("CONE", (128, 1), F32)
        self.WSC = self.sb("WSC", (128, 16), F32)
        self.ONESF = self.sb("ONESF", (128, 128), F32)
        self.HST = self.sb("HST", (128, 64), F32)
        self.PA = self.psu("PA", (128, 1024))
        self.PB = self.psu("PB", (128, 1024))
        self.PC = self.psu("PC", (128, 1024))
        self.PT = self.psu("PT", (128, 2048), BF16)

    def build(self):
        self.declare()
        with self.st:
            self.alloc_sbuf()
            self.block = self.st.enter_context(self.nc.Block())
            self.ph_init()
            if self.upto >= 1:
                self.ph_prep()
            self.xfin = "xres_a"
            if self.upto >= 2:
                self.ph_l0ffn("xres_a", "xres_b", "f1")
                self.xfin = "xres_b"
            if self.upto >= 3:
                self.ph_l0attn("xres_b", "xres_a", "a2")
                self.xfin = "xres_a"
            if self.upto >= 4:
                self.ph_l0ffn("xres_a", "xres_b", "f3")
                self.xfin = "xres_b"
            if self.upto >= 5:
                self.ph_l1ffn("xres_b", "xres_a", "l4")
                self.xfin = "xres_a"
            if self.upto >= 6:
                self.ph_l1attn("xres_a", "xres_b", "a5")
                self.xfin = "xres_b"
            if self.upto >= 7:
                self.ph_l1ffn("xres_b", "xres_a", "l6")
                self.xfin = "xres_a"
            self.ph_fini()
        return self.nc

    def ph_init(self):
        mk, block = self, self.block
        sd = self.sem("sd")
        sv = self.sem("sv")

        @block.sync
        def _(e):
            for nm, dst in (("c_ident", mk.IDENT), ("c_had128", mk.HAD128),
                            ("c_bd64", mk.BD64), ("c_bd32", mk.BD32),
                            ("c_bd16", mk.BD16), ("c_ones", mk.ONESB),
                            ("c_repl16", mk.REPL16), ("c_repl8", mk.REPL8),
                            ("c_nred", mk.NRED), ("c_h64", mk.H64S),
                            ("c_h32", mk.H32S), ("c_h16", mk.H16S)):
                sd.dinc(e.dma_start(out=dst[:], in_=mk.ap(nm)))
            for nm, dst in (("c_qnumel", mk.VEC[:, 32:40]), ("c_onesf", mk.ONESF),
                            ("c_one", mk.CONE)):
                sd.dinc(e.dma_start(out=dst[:], in_=mk.ap(nm)))
            sd.dinc(e.dma_start(out=mk.ap("xres_a"), in_=mk.ap("x_in")))
        self.d_init = sd.n

        @block.vector
        def _(e):
            e.wait_ge(sd.h, self.d_init)
            e.tensor_scalar(out=mk.CM[:], in0=mk.CONE[:], scalar1=MC2, scalar2=MC2,
                            op0=ALU.add, op1=ALU.subtract)
            e.drain()
            sv.inc(e.tensor_scalar(out=mk.CM[:], in0=mk.CM[:], scalar1=1.0,
                                   scalar2=-1.0, op0=ALU.subtract, op1=ALU.mult))
        self.v_cm = sv.n

    def ph_prep(self):
        mk, block = self, self.block
        sd = self.sem("sd")
        sg = self.sem("sg")
        sv = self.sem("sv")
        ste = self.sem("ste")
        scc = self.sem("scc")

        chunks1 = []
        for w in QUANTW:
            r, c = BIGW[w]
            rs = r // NCORE
            for r0 in range(0, rs, 128):
                for c0 in range(0, c, 1024):
                    chunks1.append((w, r0, min(128, rs - r0), c0, min(1024, c - c0)))
        svp = self.sem("svp")
        ld1 = [self.sem(f"ld1_{j}") for j in range(3)]

        @block.sync
        def _(e):
            for i, (w, r0, nr, c0, ncol) in enumerate(chunks1):
                if i >= 3:
                    e.wait_ge(svp.h, i - 2)
                buf = mk.SCR4[:, (i % 3) * 1024:(i % 3) * 1024 + ncol]
                ld1[i % 3].dinc(e.dma_start(out=buf[0:nr, :],
                                    in_=mk.ap("wsl_" + w)[r0:r0 + nr, c0:c0 + ncol]))

        @block.vector
        def _(e):
            e.memset(mk.VEC[:, 0:9], 0.0)
            e.drain()
            for i, (w, r0, nr, c0, ncol) in enumerate(chunks1):
                e.wait_ge(ld1[i % 3].h, 16 * (i // 3 + 1))
                buf = mk.SCR4[:, (i % 3) * 1024:(i % 3) * 1024 + ncol]
                qi = QPOS[w]
                e.tensor_reduce(out=mk.VEC[0:nr, 8:9], in_=buf[0:nr, :], axis=AX,
                                op=ALU.add, apply_absolute_value=True)
                e.drain()
                svp.inc(e.tensor_tensor(out=mk.VEC[0:nr, qi:qi + 1],
                                        in0=mk.VEC[0:nr, qi:qi + 1],
                                        in1=mk.VEC[0:nr, 8:9], op=ALU.add))
                e.drain()

        @block.tensor
        def _(e):
            e.wait_ge(svp.h, svp.n)
            e.wait_ge(sd.h, self.d_init)
            ste.inc(e.matmul(mk.PC[:, 0:8], mk.ONESF[:], mk.VEC[:, 0:8],
                             start=True, stop=True))
        te_abs = ste.n

        @block.vector
        def _(e):
            e.wait_ge(ste.h, te_abs)
            e.memset(mk.VEC[:, 45:48], 0.0)
            e.drain()
            sv.inc(e.tensor_copy(out=mk.VEC[:, 40:45], in_=mk.PC[:, 0:5]))
        v_col = sv.n

        @block.gpsimd
        def _(e):
            e.wait_ge(sv.h, v_col)
            sg.dinc(e.dma_start(out=mk.ap("ws_part"), in_=mk.VEC[:, 40:48]))
            e.wait_ge(sg.h, sg.n)
            scc.inc(e.collective_compute(
                "AllReduce", ALU.add, replica_groups=[list(range(NCORE))],
                ins=[mk.ap("ws_part")], outs=[mk.ap("ws_all")]))
            e.wait_ge(scc.h, scc.n)
            sg.dinc(e.dma_start(out=mk.VEC[:, 48:53], in_=mk.ap("ws_all")[:, 0:5]))
        g_sc = sg.n

        @block.vector
        def _(e):
            e.wait_ge(sg.h, g_sc)
            e.tensor_tensor(out=mk.WSC[:, 0:5], in0=mk.VEC[:, 48:53],
                            in1=mk.VEC[:, 32:37], op=ALU.mult)
            e.drain()
            e.tensor_scalar(out=mk.WSC[:, 0:5], in0=mk.WSC[:, 0:5],
                            scalar1=EPS, scalar2=None, op0=ALU.max)
            e.drain()
            sv.inc(e.reciprocal(out=mk.WSC[:, 0:5], in_=mk.WSC[:, 0:5]))
            e.drain()
            sv.inc(e.reciprocal(out=mk.WSC[:, 8:13], in_=mk.WSC[:, 0:5]))
            e.drain()
            e.memset(mk.WSC[:, 6:7], 1.0)
            e.drain()
        self.v_wsc = sv.n

        jobs = []
        for w, (r, c) in BIGW.items():
            rs = r // NCORE
            for r0 in range(0, rs, 128):
                jobs.append((w, r0, min(128, rs - r0), w in QUANTW))
        csize = 1024
        chunks2 = []
        qord = 0
        for j, (w, r0, nr, q) in enumerate(jobs):
            c = BIGW[w][1]
            for c0 in range(0, c, csize):
                chunks2.append((j, w, r0, nr, c0, min(csize, c - c0), q, qord))
                if q:
                    qord += 1
        nq_upto = []
        acc = 0
        for ch in chunks2:
            if ch[6]:
                acc += 1
            nq_upto.append(acc)
        strip_first = {}
        strip_nch = {}
        for i, ch in enumerate(chunks2):
            strip_first.setdefault(ch[0], i)
            strip_nch[ch[0]] = strip_nch.get(ch[0], 0) + 1
        ev_sq = self.sem("ev_sq")
        ev_vq = self.sem("ev_vq")
        ev_tt = self.sem("ev_tt")
        ev_se = self.sem("ev_se")
        ev_do = self.sem("ev_do")
        ld2 = [self.sem(f"ld2_{j}") for j in range(2)]

        @block.sync
        def _(e):
            for i, (j, w, r0, nr, c0, ncol, q, qo) in enumerate(chunks2):
                if i < 2:
                    e.wait_ge(svp.h, len(chunks1))
                else:
                    e.wait_ge(ev_sq.h, i - 1)
                buf = mk.SCR4[:, (i % 2) * 4096:(i % 2) * 4096 + ncol]
                ld2[i % 2].dinc(e.dma_start(out=buf[0:nr, :],
                                    in_=mk.ap("wsl_" + w)[r0:r0 + nr, c0:c0 + ncol]))

        @block.scalar
        def _(e):
            e.wait_ge(sv.h, self.v_wsc)
            e.wait_ge(sd.h, self.d_init)
            for i, (j, w, r0, nr, c0, ncol, q, qo) in enumerate(chunks2):
                e.wait_ge(ld2[i % 2].h, 16 * (i // 2 + 1))
                buf = mk.SCR4[:, (i % 2) * 4096:(i % 2) * 4096 + ncol]
                if q:
                    if qo >= 2:
                        e.wait_ge(ev_vq.h, qo - 1)
                    xs1 = mk.SCR4[:, 1024 + (qo % 2) * 4096:1024 + (qo % 2) * 4096 + ncol]
                    ev_sq.inc(e.activation(out=xs1[0:nr, :], in_=buf[0:nr, :],
                                           func=AF.Copy, bias=MC1,
                                           scale=mk.WSC[:, QPOS[w]:QPOS[w] + 1]))
                else:
                    if i >= 2:
                        e.wait_ge(ev_tt.h, i - 1)
                    bslot = mk.WST[:, 4096 + (i % 2) * 1024:4096 + (i % 2) * 1024 + ncol]
                    ev_sq.inc(e.activation(out=bslot[0:nr, :], in_=buf[0:nr, :],
                                           func=AF.Copy, bias=0.0, scale=1.0))
                if i >= 1:
                    pj, pw, pr0, pnr, pc0, pncol, pq, pqo = chunks2[i - 1]
                    e.wait_ge(ev_tt.h, i)
                    if pc0 == 0 and pj >= 1:
                        e.wait_ge(ev_do.h, 16 * pj)
                    half = ((i - 1) % 2) * 1024
                    ev_se.inc(e.copy(out=mk.SCR2[:, pc0:pc0 + pncol],
                                     in_=mk.PT[:, half:half + pncol]))
            i = len(chunks2)
            pj, pw, pr0, pnr, pc0, pncol, pq, pqo = chunks2[-1]
            e.wait_ge(ev_tt.h, i)
            half = ((i - 1) % 2) * 1024
            ev_se.inc(e.copy(out=mk.SCR2[:, pc0:pc0 + pncol],
                             in_=mk.PT[:, half:half + pncol]))

        @block.vector
        def _(e):
            for i, (j, w, r0, nr, c0, ncol, q, qo) in enumerate(chunks2):
                if not q:
                    continue
                e.wait_ge(ev_sq.h, i + 1)
                if i >= 2:
                    e.wait_ge(ev_tt.h, i - 1)
                xs1 = mk.SCR4[0:nr, 1024 + (qo % 2) * 4096:1024 + (qo % 2) * 4096 + ncol]
                t1 = mk.SCR4[0:nr, 2048:2048 + ncol]
                t2 = mk.SCR4[0:nr, 3072:3072 + ncol]
                bslot = mk.WST[0:nr, 4096 + (i % 2) * 1024:4096 + (i % 2) * 1024 + ncol]
                e.tensor_scalar(out=t1, in0=xs1, scalar1=MC1, scalar2=None,
                                op0=ALU.subtract)
                e.drain()
                e.tensor_scalar(out=t2, in0=t1, scalar1=MC2, scalar2=MC2,
                                op0=ALU.add, op1=ALU.subtract)
                e.drain()
                e.scalar_tensor_tensor(out=t1, in0=t1, scalar=mk.CM[:, 0:1],
                                       in1=t2, op0=ALU.mult, op1=ALU.add)
                e.drain()
                ev_vq.inc(e.tensor_scalar(out=bslot, in0=t1, scalar1=1.0,
                                          scalar2=-1.0, op0=ALU.min, op1=ALU.max))
                e.drain()

        @block.tensor
        def _(e):
            e.wait_ge(sd.h, self.d_init)
            for i, (j, w, r0, nr, c0, ncol, q, qo) in enumerate(chunks2):
                if q:
                    e.wait_ge(ev_vq.h, nq_upto[i])
                else:
                    e.wait_ge(ev_sq.h, i + 1)
                if i >= 2:
                    e.wait_ge(ev_se.h, i - 1)
                bslot = mk.WST[:, 4096 + (i % 2) * 1024:4096 + (i % 2) * 1024 + ncol]
                half = (i % 2) * 1024
                nb = ncol // 128
                for b in range(nb):
                    ins = e.transpose(mk.PT[:, half + b * 128:half + (b + 1) * 128],
                                      bslot[0:128, b * 128:(b + 1) * 128], mk.IDENT[:])
                    if b == nb - 1:
                        ev_tt.inc(ins)

        @block.gpsimd
        def _(e):
            for j, (w, r0, nr, q) in enumerate(jobs):
                last = strip_first[j] + strip_nch[j]
                e.wait_ge(ev_se.h, last)
                c = BIGW[w][1]
                rs = BIGW[w][0] // NCORE
                src = mk.SCR2[:, 0:c].rearrange("p (ch row) -> p ch row", row=128)
                dst = mk.ap("tsl_" + w).rearrange("(ch cl) r -> cl ch r", cl=128)[:, :, r0:r0 + nr]
                ev_do.dinc(e.dma_start(out=dst, in_=src[:, :, 0:nr]))
            e.wait_ge(ev_do.h, ev_do.n)
            for w in BIGW:
                scc.inc(e.collective_compute(
                    "AllGather", ALU.bypass, replica_groups=[list(range(NCORE))],
                    ins=[mk.ap("tsl_" + w)], outs=[mk.ap("tw_" + w)]))
        self.cc_prep = scc.n


    def emit_ln_stats(self, e, XA, cols):
        mk = self
        e.tensor_reduce(out=mk.VEC[:, 0:1], in_=XA, axis=AX, op=ALU.add)
        e.tensor_reduce(out=mk.VEC[:, 1:2], in_=XA, axis=AX, op=ALU.max)
        e.tensor_reduce(out=mk.VEC[:, 2:3], in_=XA, axis=AX, op=ALU.min)
        e.drain()

    def emit_ln_small(self, e, cols, qb=None):
        mk = self
        inv_n = 1.0 / float(cols)
        e.tensor_scalar(out=mk.VEC[:, 4:5], in0=mk.VEC[:, 0:1], scalar1=inv_n,
                        scalar2=None, op0=ALU.mult)
        e.drain()
        e.tensor_tensor(out=mk.VEC[:, 5:6], in0=mk.VEC[:, 4:5],
                        in1=mk.VEC[:, 4:5], op=ALU.mult)
        e.drain()
        e.scalar_tensor_tensor(out=mk.VEC[:, 6:7], in0=mk.VEC[:, 3:4],
                               scalar=inv_n, in1=mk.VEC[:, 5:6],
                               op0=ALU.mult, op1=ALU.subtract)
        e.drain()
        ins = e.tensor_scalar(out=mk.VEC[:, 6:7], in0=mk.VEC[:, 6:7],
                              scalar1=EPS, scalar2=None, op0=ALU.add)
        e.drain()
        return ins

    def emit_ln_small2(self, e, cols, qb):
        mk = self
        e.reciprocal(out=mk.VEC[:, 8:9], in_=mk.VEC[:, 7:8])
        e.drain()
        if qb is None:
            ins = e.scalar_tensor_tensor(out=mk.VEC[:, 11:12], in0=mk.VEC[:, 4:5],
                                   scalar=-1.0, in1=mk.VEC[:, 8:9],
                                   op0=ALU.mult, op1=ALU.mult)
            e.drain()
            return ins
        e.tensor_tensor(out=mk.VEC[:, 9:10], in0=mk.VEC[:, 1:2],
                        in1=mk.VEC[:, 4:5], op=ALU.subtract)
        e.tensor_tensor(out=mk.VEC[:, 13:14], in0=mk.VEC[:, 4:5],
                        in1=mk.VEC[:, 2:3], op=ALU.subtract)
        e.drain()
        e.tensor_tensor(out=mk.VEC[:, 9:10], in0=mk.VEC[:, 9:10],
                        in1=mk.VEC[:, 13:14], op=ALU.max)
        e.drain()
        e.tensor_tensor(out=mk.VEC[:, 9:10], in0=mk.VEC[:, 9:10],
                        in1=mk.VEC[:, 8:9], op=ALU.mult)
        e.drain()
        e.tensor_scalar(out=mk.VEC[:, 9:10], in0=mk.VEC[:, 9:10],
                        scalar1=EPS, scalar2=None, op0=ALU.max)
        e.drain()
        e.tensor_scalar(out=mk.VEC[:, 12:13], in0=mk.VEC[:, 9:10],
                        scalar1=1.0 / qb, scalar2=None, op0=ALU.mult)
        e.reciprocal(out=mk.VEC[:, 10:11], in_=mk.VEC[:, 9:10])
        e.drain()
        e.tensor_scalar(out=mk.VEC[:, 10:11], in0=mk.VEC[:, 10:11],
                        scalar1=qb, scalar2=None, op0=ALU.mult)
        e.drain()
        e.tensor_tensor(out=mk.VEC[:, 10:11], in0=mk.VEC[:, 10:11],
                        in1=mk.VEC[:, 8:9], op=ALU.mult)
        e.drain()
        e.scalar_tensor_tensor(out=mk.VEC[:, 11:12], in0=mk.VEC[:, 4:5],
                               scalar=-1.0, in1=mk.VEC[:, 10:11],
                               op0=ALU.mult, op1=ALU.mult)
        e.drain()
        ins = e.tensor_scalar(out=mk.VEC[:, 11:12], in0=mk.VEC[:, 11:12],
                        scalar1=MC1, scalar2=None, op0=ALU.add)
        e.drain()
        return ins

    def emit_round_chain(self, e, xs1, t1, t2, out_bf, lo, hi, inv_col, bias_col=None):
        mk = self
        if bias_col is None:
            e.tensor_scalar(out=t1, in0=xs1, scalar1=MC1, scalar2=None, op0=ALU.subtract)
        else:
            e.tensor_scalar(out=t1, in0=xs1, scalar1=bias_col, scalar2=MC1,
                            op0=ALU.add, op1=ALU.subtract)
        e.drain()
        e.tensor_scalar(out=t2, in0=t1, scalar1=MC2, scalar2=MC2,
                        op0=ALU.add, op1=ALU.subtract)
        e.drain()
        e.scalar_tensor_tensor(out=t1, in0=t1, scalar=mk.CM[:, 0:1], in1=t2,
                               op0=ALU.mult, op1=ALU.add)
        e.drain()
        e.tensor_scalar(out=t1, in0=t1, scalar1=float(hi), scalar2=float(lo),
                        op0=ALU.min, op1=ALU.max)
        e.drain()
        ins = e.tensor_scalar(out=out_bf, in0=t1, scalar1=inv_col, scalar2=None,
                        op0=ALU.mult)
        e.drain()
        return ins

    def emit_fwht_block(self, tag, src_dram, nfeat, nb, qb, lo, hi, u_dram,
                        v_dram, qd_dram, wint, tmajor_src=False):
        mk, block = self, self.block
        ISQ = float(1.0 / np.sqrt(np.float64(nfeat)))
        HNB = {64: mk.H64S, 32: mk.H32S, 16: mk.H16S}[nb]
        NG = nb // 4
        fic = self.semv("fic")
        fw = self.semv("fw")
        fvt = self.semv("fvt")
        wu0 = [self.semv("wu0" + str(j)) for j in range(2)]
        wv0 = [self.semv("wv0" + str(j)) for j in range(2)]
        wqd = self.semv("wqd")
        tC1 = self.semv("tC1")
        vC1 = self.semv("vC1")
        tC2 = self.semv("tC2")
        vC2 = self.semv("vC2")
        vQ = self.semv("vQ")
        sQ = self.semv("sQ")
        sv = self.sem("sv")
        fence = self.snapshot()
        sl, sl_lo = src_dram, src_dram + "_lo"
        ICH = mk.SCR2[:, 0:nfeat]
        ICL = mk.SLAB[:, 0:nfeat]
        UTH = lambda s: mk.SLAB[:, 8192 + s * 512:8192 + s * 512 + 512]
        UTL = lambda s: mk.SLAB[:, 9216 + s * 512:9216 + s * 512 + 512]
        WH = mk.SLAB[0:nb, 10240:18432]
        WL = mk.SLAB[0:nb, 18432:26624]
        VTH = mk.SLAB[:, 26624:26624 + nfeat]
        VTL = mk.SLAB[:, 34816:34816 + nfeat]
        T1 = mk.SLAB[:, 26624:26624 + nfeat]
        T2 = mk.SLAB[:, 34816:34816 + nfeat]
        QDT = mk.SLAB[:, 43008:43008 + nfeat]
        V2H = lambda s: mk.SLAB[0:nb, 51200 + s * 512:51200 + s * 512 + 512]
        V2L = lambda s: mk.SLAB[0:nb, 52224 + s * 512:52224 + s * 512 + 512]

        @block.sync
        def _(e):
            self.emit_fence(e, fence)
            for s in wint:
                e.wait_ge(s.h, s.n)
            for tc in range(8):
                if tc >= 1:
                    e.wait_ge(tC1.h, tC1.b + NG * tc)
                if tmajor_src:
                    for bq in range(nb):
                        fic.dinc(e.dma_start_transpose(
                            out=ICH[:, bq * 128:(bq + 1) * 128],
                            in_=mk.ap(sl)[tc * 128:(tc + 1) * 128, bq * 128:(bq + 1) * 128]))
                        fic.dinc(e.dma_start_transpose(
                            out=ICL[:, bq * 128:(bq + 1) * 128],
                            in_=mk.ap(sl_lo)[tc * 128:(tc + 1) * 128, bq * 128:(bq + 1) * 128]))
                else:
                    for nm, dst in ((sl, ICH), (sl_lo, ICL)):
                        s3 = mk.ap(nm).rearrange("(b p) t -> p b t", p=128)
                        d3 = dst.rearrange("p (b t) -> p b t", t=128)
                        fic.dinc(e.dma_start(out=d3, in_=s3[:, :, tc * 128:(tc + 1) * 128]))
                for half in range(2):
                    e.wait_ge(wu0[0].h, wu0[0].b + 32 * NG * (tc + 1) // 2)
                    e.wait_ge(wu0[1].h, wu0[1].b + 32 * NG * (tc + 1) // 2)
                    if tc > 0 or half > 0:
                        e.wait_ge(tC2.h, tC2.b + 16 * (2 * tc + half))
                    for ud, wdst in ((u_dram, WH), (u_dram + "_lo", WL)):
                        srcu = mk.ap(ud).rearrange("t (b p) -> b t p", p=128)
                        dstw = wdst.rearrange("b (t p) -> b t p", p=128)
                        fw.dinc(e.dma_start(
                            out=dstw, in_=srcu[:, tc * 128 + half * 64:tc * 128 + half * 64 + 64, :]))
                e.wait_ge(wv0[0].h, wv0[0].b + 32 * 16 * (tc + 1))
                e.wait_ge(wv0[1].h, wv0[1].b + 32 * 16 * (tc + 1))
                if tc >= 1:
                    e.wait_ge(sQ.h, sQ.b + tc)
                    e.wait_ge(vQ.h, vQ.b + 3 * tc)
                fvt.dinc(e.dma_start(out=VTH, in_=mk.ap(v_dram)[tc * 128:(tc + 1) * 128, :]))
                fvt.dinc(e.dma_start(out=VTL, in_=mk.ap(v_dram + "_lo")[tc * 128:(tc + 1) * 128, :]))

        @block.tensor
        def _(e):
            self.emit_fence(e, fence)
            for tc in range(8):
                e.wait_ge(fic.h, fic.b + 16 * 2 * (nb if tmajor_src else 1) * (tc + 1))
                for grp in range(NG):
                    gi = tc * NG + grp
                    if gi >= 2:
                        e.wait_ge(vC1.h, vC1.b + gi - 1)
                    for bb in range(4):
                        bq = grp * 4 + bb
                        PCD = mk.PC[:, (grp % 2) * 512 + bb * 128:(grp % 2) * 512 + (bb + 1) * 128]
                        e.matmul(PCD, ICH[:, bq * 128:(bq + 1) * 128], mk.HAD128[:],
                                 start=True, stop=False, skip_group_check=True)
                        ins = e.matmul(PCD, ICL[:, bq * 128:(bq + 1) * 128], mk.HAD128[:],
                                       start=False, stop=True, skip_group_check=True)
                    tC1.inc(ins)
                for half in range(2):
                    hi_ = tc * 2 + half
                    e.wait_ge(fw.h, fw.b + 16 * 2 * (hi_ + 1))
                    for n in range(16):
                        ni = hi_ * 16 + n
                        if ni >= 2:
                            e.wait_ge(vC2.h, vC2.b + ni - 1)
                        PAD = mk.PA[0:nb, (n % 2) * 512:(n % 2) * 512 + 512]
                        e.matmul(PAD, HNB[:], WH[:, n * 512:(n + 1) * 512],
                                 start=True, stop=False, skip_group_check=True)
                        ins = e.matmul(PAD, HNB[:], WL[:, n * 512:(n + 1) * 512],
                                       start=False, stop=True, skip_group_check=True)
                        tC2.inc(ins)

        @block.vector
        def _(e):
            self.emit_fence(e, fence)
            for tc in range(8):
                for grp in range(NG):
                    gi = tc * NG + grp
                    e.wait_ge(tC1.h, tC1.b + gi + 1)
                    if gi >= 2:
                        e.wait_ge(wu0[gi % 2].h, wu0[gi % 2].b + 32 * (gi // 2))
                    PCS = mk.PC[:, (grp % 2) * 512:(grp % 2) * 512 + 512]
                    e.tensor_copy(out=UTH(gi % 2), in_=PCS)
                    e.drain()
                    vC1.inc(e.tensor_tensor(out=UTL(gi % 2), in0=PCS,
                                            in1=UTH(gi % 2), op=ALU.subtract))
                    e.drain()
                for half in range(2):
                    for n in range(16):
                        ni = (tc * 2 + half) * 16 + n
                        e.wait_ge(tC2.h, tC2.b + ni + 1)
                        if ni >= 2:
                            e.wait_ge(wv0[ni % 2].h, wv0[ni % 2].b + 32 * (ni // 2))
                        PAS = mk.PA[0:nb, (n % 2) * 512:(n % 2) * 512 + 512]
                        e.tensor_copy(out=V2H(ni % 2), in_=PAS)
                        e.drain()
                        vC2.inc(e.tensor_tensor(out=V2L(ni % 2), in0=PAS,
                                                in1=V2H(ni % 2), op=ALU.subtract))
                        e.drain()
                e.wait_ge(fvt.h, fvt.b + 32 * (tc + 1))
                vQ.inc(e.tensor_tensor(out=mk.SCR4[:, 0:nfeat], in0=VTH, in1=VTL,
                                       op=ALU.add))
                e.drain()
                e.tensor_reduce(out=mk.VEC[:, 20:21], in_=mk.SCR4[:, 0:nfeat],
                                axis=AX, op=ALU.max, apply_absolute_value=True)
                e.drain()
                e.tensor_scalar(out=mk.VEC[:, 21:22], in0=mk.VEC[:, 20:21],
                                scalar1=ISQ, scalar2=EPS, op0=ALU.mult, op1=ALU.max)
                e.drain()
                e.tensor_scalar(out=mk.VEC[:, 23:24], in0=mk.VEC[:, 21:22],
                                scalar1=1.0 / qb, scalar2=None, op0=ALU.mult)
                e.reciprocal(out=mk.VEC[:, 22:23], in_=mk.VEC[:, 21:22])
                e.drain()
                vQ.inc(e.tensor_scalar(out=mk.VEC[:, 22:23], in0=mk.VEC[:, 22:23],
                                       scalar1=qb * ISQ, scalar2=None, op0=ALU.mult))
                e.wait_ge(sQ.h, sQ.b + tc + 1)
                if tc >= 1:
                    e.wait_ge(wqd.h, wqd.b + 16 * tc)
                vQ.inc(self.emit_round_chain(e, mk.SCR4[:, 0:nfeat], T1, T2, QDT,
                                             lo, hi, mk.VEC[:, 23:24]))

        @block.scalar
        def _(e):
            self.emit_fence(e, fence)
            for tc in range(8):
                e.wait_ge(vQ.h, vQ.b + 3 * tc + 2)
                sQ.inc(e.activation(out=mk.SCR4[:, 0:nfeat], in_=mk.SCR4[:, 0:nfeat],
                                    func=AF.Copy, bias=MC1,
                                    scale=mk.VEC[:, 22:23]))

        @block.gpsimd
        def _(e):
            self.emit_fence(e, fence)
            for tc in range(8):
                for grp in range(NG):
                    gi = tc * NG + grp
                    e.wait_ge(vC1.h, vC1.b + gi + 1)
                    for ud, slot in ((u_dram, UTH), (u_dram + "_lo", UTL)):
                        wu0[gi % 2].dinc(e.dma_start(
                            out=mk.ap(ud)[tc * 128:(tc + 1) * 128, grp * 512:(grp + 1) * 512],
                            in_=slot(gi % 2)))
                for half in range(2):
                    for n in range(16):
                        ni = (tc * 2 + half) * 16 + n
                        e.wait_ge(vC2.h, vC2.b + ni + 1)
                        t0 = tc * 128 + half * 64 + n * 4
                        for vd, slot in ((v_dram, V2H), (v_dram + "_lo", V2L)):
                            dstv = mk.ap(vd).rearrange("t (a p) -> a t p", p=128)
                            wv0[ni % 2].dinc(e.dma_start(
                                out=dstv[:, t0:t0 + 4, :],
                                in_=slot(ni % 2).rearrange("a (t p) -> a t p", p=128)))
                e.wait_ge(vQ.h, vQ.b + 3 * tc + 3)
                wqd.dinc(e.dma_start(out=mk.ap(qd_dram)[tc * 128:(tc + 1) * 128, :],
                                     in_=QDT))

    def emit_down_block(self, tag, qd_dram, nk, tw_name, jdim, msc_idx,
                        scale_mult, xin, xout):
        mk, block = self, self.block
        sv = self.sem("sv")
        fqf = self.semv("fqf")
        fwd = [self.semv("fwd" + str(j)) for j in range(2)]
        fxr = [self.semv("fxr" + str(j)) for j in range(2)]
        wqd = self.semv("wqd")
        wxo = [self.semv("wxo" + str(j)) for j in range(2)]
        tD = self.semv("tD")
        vD = self.semv("vD")
        fence = self.snapshot()
        @block.sync
        def _(e):
            self.emit_fence(e, fence)
            e.wait_ge(wqd.h, wqd.n)
            for k in range(nk):
                fqf.dinc(e.dma_start_transpose(
                    out=mk.SLAB[:, k * 1024:(k + 1) * 1024],
                    in_=mk.ap(qd_dram)[:, k * 128:(k + 1) * 128]))
            for half in range(2):
                for mp in range(4):
                    for mm_ in range(2):
                        m = mp * 2 + mm_
                        i = (half * 4 + mp) * 2 + mm_
                        if i >= 2:
                            e.wait_ge(vD.h, vD.b + i - 1)
                        fxr[i % 2].dinc(e.dma_start(
                            out=mk.SCR4[:, (i % 2) * 1024:(i % 2) * 1024 + 1024],
                            in_=mk.ap(xin)[m * 128:(m + 1) * 128, half * 1024:half * 1024 + 1024]))
                    for k in range(nk):
                        ki = (half * 4 + mp) * nk + k
                        if ki >= 2:
                            e.wait_ge(tD.h, tD.b + ki - 1)
                        srcw = mk.ap(tw_name).rearrange(
                            "(rb j) c -> j rb c", j=jdim)[k * 128:(k + 1) * 128, half * 4:half * 4 + 4, :]
                        dstw = mk.WST[:, (k % 2) * 1024:(k % 2) * 1024 + 1024].rearrange(
                            "p (rb c) -> p rb c", c=256)
                        fwd[k % 2].dinc(e.dma_start(out=dstw, in_=srcw))

        @block.tensor
        def _(e):
            self.emit_fence(e, fence)
            e.wait_ge(fqf.h, fqf.b + 16 * nk)
            for half in range(2):
                for mp in range(4):
                    for k in range(nk):
                        ki = (half * 4 + mp) * nk + k
                        e.wait_ge(fwd[k % 2].h, fwd[k % 2].b + 16 * (ki // 2 + 1))
                        if k == 0 and (half * 4 + mp) >= 1:
                            e.wait_ge(vD.h, vD.b + 2 * (half * 4 + mp))
                        for mm_ in range(2):
                            m = mp * 2 + mm_
                            PSUM = mk.PA if mm_ == 0 else mk.PB
                            QK = mk.SLAB[:, k * 1024 + m * 128:k * 1024 + (m + 1) * 128]
                            for rb in range(4):
                                ins = e.matmul(
                                    PSUM[:, rb * 256:(rb + 1) * 256], QK,
                                    mk.WST[:, (k % 2) * 1024 + rb * 256:(k % 2) * 1024 + (rb + 1) * 256],
                                    start=(k == 0 and rb % 2 == 0),
                                    stop=(k == nk - 1), skip_group_check=True)
                        tD.inc(ins)

        @block.vector
        def _(e):
            self.emit_fence(e, fence)
            e.wait_ge(sv.h, self.v_wsc)
            e.tensor_scalar(out=mk.WSC[:, 5:6], in0=mk.WSC[:, msc_idx:msc_idx + 1],
                            scalar1=scale_mult, scalar2=None, op0=ALU.mult)
            e.drain()
            for half in range(2):
                for mp in range(4):
                    e.wait_ge(tD.h, tD.b + (half * 4 + mp + 1) * nk)
                    for mm_ in range(2):
                        m = mp * 2 + mm_
                        i = (half * 4 + mp) * 2 + mm_
                        PSUM = mk.PA if mm_ == 0 else mk.PB
                        e.wait_ge(fxr[i % 2].h, fxr[i % 2].b + 16 * (i // 2 + 1))
                        if i >= 2:
                            e.wait_ge(wxo[i % 2].h, wxo[i % 2].b + 16 * (i // 2))
                        XO = mk.SCR4[:, 2048 + (i % 2) * 1024:2048 + (i % 2) * 1024 + 1024]
                        vD.inc(e.scalar_tensor_tensor(
                            out=XO, in0=PSUM[:, 0:1024], scalar=mk.WSC[:, 5:6],
                            in1=mk.SCR4[:, (i % 2) * 1024:(i % 2) * 1024 + 1024],
                            op0=ALU.mult, op1=ALU.add))

        @block.gpsimd
        def _(e):
            self.emit_fence(e, fence)
            for half in range(2):
                for mp in range(4):
                    for mm_ in range(2):
                        m = mp * 2 + mm_
                        i = (half * 4 + mp) * 2 + mm_
                        e.wait_ge(vD.h, vD.b + i + 1)
                        XO = mk.SCR4[:, 2048 + (i % 2) * 1024:2048 + (i % 2) * 1024 + 1024]
                        wxo[i % 2].dinc(e.dma_start(
                            out=mk.ap(xout)[m * 128:(m + 1) * 128, half * 1024:half * 1024 + 1024],
                            in_=XO))

    def ph_l0ffn(self, xin, xout, tag):
        mk, block = self, self.block
        sv = self.sem("sv")
        ISQ = float(1.0 / np.sqrt(np.float64(I0)))
        fx = [self.semv("fx" + str(j)) for j in range(2)]
        fqt = self.semv("fqt")
        fg = [self.semv("fg" + str(j)) for j in range(2)]
        fu = [self.semv("fu" + str(j)) for j in range(2)]
        fic = self.semv("fic")
        fw = self.semv("fw")
        fvt = self.semv("fvt")
        fqf = self.semv("fqf")
        fwd = [self.semv("fwd" + str(j)) for j in range(2)]
        fxr = [self.semv("fxr" + str(j)) for j in range(2)]
        wqx = [self.semv("wqx" + str(j)) for j in range(2)]
        wint = [self.semv("wint" + str(j)) for j in range(2)]
        wu0 = [self.semv("wu0" + str(j)) for j in range(2)]
        wv0 = [self.semv("wv0" + str(j)) for j in range(2)]
        wqd = self.semv("wqd")
        wxo = [self.semv("wxo" + str(j)) for j in range(2)]
        vA = self.semv("vA")
        sA = self.semv("sA")
        vB = self.semv("vB")
        sB = self.semv("sB")
        tB = self.semv("tB")
        tC1 = self.semv("tC1")
        vC1 = self.semv("vC1")
        tC2 = self.semv("tC2")
        vC2 = self.semv("vC2")
        vQ = self.semv("vQ")
        sQ = self.semv("sQ")
        tD = self.semv("tD")
        vD = self.semv("vD")
        fence = self.snapshot()
        fence = self.snapshot()

        @block.sync
        def _(e):
            self.emit_fence(e, fence)
            for m in range(8):
                if m >= 2:
                    e.wait_ge(vA.h, vA.b + 3 * (m - 2) + 3)
                fx[m % 2].dinc(e.dma_start(
                    out=mk.SCR4[:, (m % 2) * 2048:(m % 2) * 2048 + 2048],
                    in_=mk.ap(xin)[m * 128:(m + 1) * 128, :]))

        @block.scalar
        def _(e):
            self.emit_fence(e, fence)
            for m in range(8):
                XA = mk.SCR4[:, (m % 2) * 2048:(m % 2) * 2048 + 2048]
                e.wait_ge(fx[m % 2].h, fx[m % 2].b + 16 * (m // 2 + 1))
                if m >= 1:
                    e.wait_ge(vA.h, vA.b + 3 * (m - 1) + 1)
                sA.inc(e.activation(out=mk.SCR2[:, 6144:8192], in_=XA,
                                    func=AF.Square, bias=0.0, scale=1.0,
                                    accum_out=mk.VEC[:, 3:4]))
                e.wait_ge(vA.h, vA.b + 3 * m + 1)
                sA.inc(e.activation(out=mk.VEC[:, 7:8], in_=mk.VEC[:, 6:7],
                                    func=AF.Sqrt, bias=0.0, scale=1.0))
                e.wait_ge(vA.h, vA.b + 3 * m + 2)
                sA.inc(e.activation(out=mk.SCR4[:, 4096:6144], in_=XA,
                                    func=AF.Copy, bias=0.0,
                                    scale=mk.VEC[:, 10:11]))

        @block.vector
        def _(e):
            self.emit_fence(e, fence)
            for m in range(8):
                XA = mk.SCR4[:, (m % 2) * 2048:(m % 2) * 2048 + 2048]
                e.wait_ge(fx[m % 2].h, fx[m % 2].b + 16 * (m // 2 + 1))
                e.wait_ge(sA.h, sA.b + 3 * m + 1)
                self.emit_ln_stats(e, XA, 2048)
                vA.inc(self.emit_ln_small(e, 2048))
                e.wait_ge(sA.h, sA.b + 3 * m + 2)
                vA.inc(self.emit_ln_small2(e, 2048, 128.0))
                e.wait_ge(sA.h, sA.b + 3 * m + 3)
                if m >= 2:
                    e.wait_ge(wqx[m % 2].h, wqx[m % 2].b + 16 * (m // 2))
                QB = mk.SCR2[:, (m % 2) * 2048:(m % 2) * 2048 + 2048]
                vA.inc(self.emit_round_chain(e, mk.SCR4[:, 4096:6144],
                                             mk.SCR4[:, 6144:8192], XA, QB,
                                             -128.0, 127.0, mk.VEC[:, 12:13],
                                             bias_col=mk.VEC[:, 11:12]))

        @block.gpsimd
        def _(e):
            self.emit_fence(e, fence)
            for m in range(8):
                e.wait_ge(vA.h, vA.b + 3 * m + 3)
                QB = mk.SCR2[:, (m % 2) * 2048:(m % 2) * 2048 + 2048]
                wqx[m % 2].dinc(e.dma_start(
                    out=mk.ap("qx0")[m * 128:(m + 1) * 128, :], in_=QB))

        QTS = mk.SLAB[:, 0:16384]
        @block.sync
        def _(e):
            e.wait_ge(wqx[0].h, wqx[0].n)
            e.wait_ge(wqx[1].h, wqx[1].n)
            for k in range(16):
                fqt.dinc(e.dma_start_transpose(
                    out=QTS[:, k * 1024:(k + 1) * 1024],
                    in_=mk.ap("qx0")[:, k * 128:(k + 1) * 128]))
            for i in range(64):
                rb, c0 = i // 8, (i % 8) * 128
                if i >= 2:
                    e.wait_ge(tB.h, tB.b + 2 * (i - 1))
                for w, fslot, off in (("gate", fg, 0), ("up", fu, 4096)):
                    srcw = mk.ap("tw_" + w).rearrange(
                        "(rb kp) c -> kp rb c", kp=2048)[:, rb, c0:c0 + 128]
                    srcw = srcw.rearrange("(kc p) c -> p kc c", p=128)
                    dstw = mk.WST[:, off + (i % 2) * 2048:off + (i % 2) * 2048 + 2048]
                    dstw = dstw.rearrange("p (kc c) -> p kc c", c=128)
                    fslot[i % 2].dinc(e.dma_start(out=dstw, in_=srcw))

        @block.tensor
        def _(e):
            self.emit_fence(e, fence)
            e.wait_ge(fqt.h, fqt.n)
            for i in range(64):
                e.wait_ge(fg[i % 2].h, fg[i % 2].b + 16 * (i // 2 + 1))
                if i >= 1:
                    e.wait_ge(sB.h, sB.b + i)
                for k in range(16):
                    GW = mk.WST[:, (i % 2) * 2048 + k * 128:(i % 2) * 2048 + (k + 1) * 128]
                    e.matmul(mk.PA[:, 0:512], GW, QTS[:, k * 1024:k * 1024 + 512],
                             start=(k == 0), stop=(k == 15))
                    ins = e.matmul(mk.PA[:, 512:1024], GW,
                                   QTS[:, k * 1024 + 512:(k + 1) * 1024],
                                   start=(k == 0), stop=(k == 15))
                tB.inc(ins)
                e.wait_ge(fu[i % 2].h, fu[i % 2].b + 16 * (i // 2 + 1))
                if i >= 1:
                    e.wait_ge(vB.h, vB.b + i)
                for k in range(16):
                    UW = mk.WST[:, 4096 + (i % 2) * 2048 + k * 128:4096 + (i % 2) * 2048 + (k + 1) * 128]
                    e.matmul(mk.PB[:, 0:512], UW, QTS[:, k * 1024:k * 1024 + 512],
                             start=(k == 0), stop=(k == 15))
                    ins = e.matmul(mk.PB[:, 512:1024], UW,
                                   QTS[:, k * 1024 + 512:(k + 1) * 1024],
                                   start=(k == 0), stop=(k == 15))
                tB.inc(ins)

        @block.scalar
        def _(e):
            for i in range(64):
                e.wait_ge(tB.h, tB.b + 2 * i + 1)
                if i >= 1:
                    e.wait_ge(vB.h, vB.b + i)
                sB.inc(e.activation(out=mk.SCR4[:, 0:1024], in_=mk.PA[:, 0:1024],
                                    func=AF.Silu, bias=0.0,
                                    scale=mk.WSC[:, 8:9]))

        @block.vector
        def _(e):
            for i in range(64):
                e.wait_ge(tB.h, tB.b + 2 * i + 2)
                e.wait_ge(sB.h, sB.b + i + 1)
                if i >= 2:
                    e.wait_ge(wint[i % 2].h, wint[i % 2].b + 32 * (i // 2))
                INTB = mk.SCR2[:, (i % 2) * 1024:(i % 2) * 1024 + 1024]
                INTL = mk.SCR2[:, 2048 + (i % 2) * 1024:2048 + (i % 2) * 1024 + 1024]
                e.scalar_tensor_tensor(out=mk.SCR4[:, 1024:2048], in0=mk.PB[:, 0:1024],
                                       scalar=mk.WSC[:, 9:10],
                                       in1=mk.SCR4[:, 0:1024],
                                       op0=ALU.mult, op1=ALU.mult)
                e.drain()
                e.tensor_copy(out=INTB, in_=mk.SCR4[:, 1024:2048])
                e.drain()
                vB.inc(e.tensor_tensor(out=INTL, in0=mk.SCR4[:, 1024:2048],
                                       in1=INTB, op=ALU.subtract))
                e.drain()

        @block.gpsimd
        def _(e):
            for i in range(64):
                e.wait_ge(vB.h, vB.b + i + 1)
                INTB = mk.SCR2[:, (i % 2) * 1024:(i % 2) * 1024 + 1024]
                INTL = mk.SCR2[:, 2048 + (i % 2) * 1024:2048 + (i % 2) * 1024 + 1024]
                wint[i % 2].dinc(e.dma_start(
                    out=mk.ap("int0")[i * 128:(i + 1) * 128, :], in_=INTB))
                wint[i % 2].dinc(e.dma_start(
                    out=mk.ap("int0_lo")[i * 128:(i + 1) * 128, :], in_=INTL))

        self.emit_fwht_block(tag + "c", "int0", I0, 64, 128.0, -128.0, 127.0,
                             "u0", "v0", "qd0", wint)
        self.emit_down_block(tag + "d", "qd0", 64, "tw_down", I0, 10, 0.5,
                             xin, xout)
        self.ffn_wxo = wxo

    def ph_l1ffn(self, xin, xout, tag):
        mk, block = self, self.block
        sv = self.sem("sv")
        fence = self.snapshot()
        fx = [self.semv("fx" + str(j)) for j in range(2)]
        wqx = [self.semv("wqx" + str(j)) for j in range(2)]
        vA = self.semv("vA")
        sA = self.semv("sA")
        fqt = self.semv("fqt")
        fu = [self.semv("fu" + str(j)) for j in range(2)]
        wint = [self.semv("wint" + str(j)) for j in range(2)]
        vB = self.semv("vB")
        sB = self.semv("sB")
        tB = self.semv("tB")

        @block.sync
        def _(e):
            self.emit_fence(e, fence)
            for m in range(8):
                if m >= 2:
                    e.wait_ge(vA.h, vA.b + 3 * (m - 2) + 3)
                fx[m % 2].dinc(e.dma_start(
                    out=mk.SCR4[:, (m % 2) * 2048:(m % 2) * 2048 + 2048],
                    in_=mk.ap(xin)[m * 128:(m + 1) * 128, :]))

        @block.scalar
        def _(e):
            self.emit_fence(e, fence)
            for m in range(8):
                XA = mk.SCR4[:, (m % 2) * 2048:(m % 2) * 2048 + 2048]
                e.wait_ge(fx[m % 2].h, fx[m % 2].b + 16 * (m // 2 + 1))
                if m >= 1:
                    e.wait_ge(vA.h, vA.b + 3 * (m - 1) + 1)
                sA.inc(e.activation(out=mk.SCR4[:, 6144:8192], in_=XA,
                                    func=AF.Square, bias=0.0, scale=1.0,
                                    accum_out=mk.VEC[:, 3:4]))
                e.wait_ge(vA.h, vA.b + 3 * m + 1)
                sA.inc(e.activation(out=mk.VEC[:, 7:8], in_=mk.VEC[:, 6:7],
                                    func=AF.Sqrt, bias=0.0, scale=1.0))

        @block.vector
        def _(e):
            self.emit_fence(e, fence)
            for m in range(8):
                XA = mk.SCR4[:, (m % 2) * 2048:(m % 2) * 2048 + 2048]
                e.wait_ge(fx[m % 2].h, fx[m % 2].b + 16 * (m // 2 + 1))
                e.wait_ge(sA.h, sA.b + 2 * m + 1)
                self.emit_ln_stats(e, XA, 2048)
                vA.inc(self.emit_ln_small(e, 2048))
                e.wait_ge(sA.h, sA.b + 2 * m + 2)
                vA.inc(self.emit_ln_small2(e, 2048, None))
                if m >= 2:
                    e.wait_ge(wqx[m % 2].h, wqx[m % 2].b + 32 * (m // 2))
                XLNB = mk.SCR2[:, (m % 2) * 2048:(m % 2) * 2048 + 2048]
                XLNLO = mk.SCR2[:, 4096 + (m % 2) * 2048:4096 + (m % 2) * 2048 + 2048]
                e.tensor_scalar(out=mk.SCR4[:, 4096:6144], in0=XA,
                                scalar1=mk.VEC[:, 8:9], scalar2=mk.VEC[:, 11:12],
                                op0=ALU.mult, op1=ALU.add)
                e.drain()
                e.tensor_copy(out=XLNB, in_=mk.SCR4[:, 4096:6144])
                e.drain()
                vA.inc(e.tensor_tensor(out=XLNLO, in0=mk.SCR4[:, 4096:6144],
                                       in1=XLNB, op=ALU.subtract))
                e.drain()

        @block.gpsimd
        def _(e):
            self.emit_fence(e, fence)
            for m in range(8):
                e.wait_ge(vA.h, vA.b + 3 * m + 3)
                XLNB = mk.SCR2[:, (m % 2) * 2048:(m % 2) * 2048 + 2048]
                XLNLO = mk.SCR2[:, 4096 + (m % 2) * 2048:4096 + (m % 2) * 2048 + 2048]
                wqx[m % 2].dinc(e.dma_start(
                    out=mk.ap("qx0")[m * 128:(m + 1) * 128, :], in_=XLNB))
                wqx[m % 2].dinc(e.dma_start(
                    out=mk.ap("qx0_lo")[m * 128:(m + 1) * 128, :], in_=XLNLO))

        self.emit_fwht_block(tag + "a", "qx0", H, 16, 127.0, -127.0, 127.0,
                             "uh", "vh", "qdh", wqx, tmajor_src=True)

        awqd = self.semv("wqd")
        @block.sync
        def _(e):
            e.wait_ge(awqd.h, awqd.b)
            for k in range(16):
                fqt.dinc(e.dma_start_transpose(
                    out=mk.SLAB[:, k * 1024:(k + 1) * 1024],
                    in_=mk.ap("qdh")[:, k * 128:(k + 1) * 128]))
            for i in range(32):
                rb, c0 = i // 4, (i % 4) * 128
                if i >= 2:
                    e.wait_ge(tB.h, tB.b + i - 1)
                srcw = mk.ap("tw_l1up").rearrange(
                    "(rb kp) c -> kp rb c", kp=2048)[:, rb, c0:c0 + 128]
                srcw = srcw.rearrange("(kc p) c -> p kc c", p=128)
                dstw = mk.WST[:, (i % 2) * 2048:(i % 2) * 2048 + 2048]
                dstw = dstw.rearrange("p (kc c) -> p kc c", c=128)
                fu[i % 2].dinc(e.dma_start(out=dstw, in_=srcw))

        @block.tensor
        def _(e):
            e.wait_ge(fqt.h, fqt.b + 16 * 16)
            for i in range(32):
                e.wait_ge(fu[i % 2].h, fu[i % 2].b + 16 * (i // 2 + 1))
                if i >= 1:
                    e.wait_ge(vB.h, vB.b + i)
                for k in range(16):
                    UW = mk.WST[:, (i % 2) * 2048 + k * 128:(i % 2) * 2048 + (k + 1) * 128]
                    e.matmul(mk.PA[:, 0:512], UW, mk.SLAB[:, k * 1024:k * 1024 + 512],
                             start=(k == 0), stop=(k == 15))
                    ins = e.matmul(mk.PA[:, 512:1024], UW,
                                   mk.SLAB[:, k * 1024 + 512:(k + 1) * 1024],
                                   start=(k == 0), stop=(k == 15))
                tB.inc(ins)

        vBB = self.semv("vBB")
        @block.vector
        def _(e):
            for i in range(32):
                e.wait_ge(tB.h, tB.b + i + 1)
                if i >= 1:
                    e.wait_ge(sB.h, sB.b + i)
                vB.inc(e.tensor_scalar(out=mk.SCR4[:, 0:1024], in0=mk.PA[:, 0:1024],
                                       scalar1=mk.WSC[:, 11:12], scalar2=0.0,
                                       op0=ALU.mult, op1=ALU.max))
                e.drain()
                e.wait_ge(sB.h, sB.b + i + 1)
                if i >= 2:
                    e.wait_ge(wint[i % 2].h, wint[i % 2].b + 32 * (i // 2))
                INTB = mk.SCR2[:, (i % 2) * 1024:(i % 2) * 1024 + 1024]
                INTL = mk.SCR2[:, 2048 + (i % 2) * 1024:2048 + (i % 2) * 1024 + 1024]
                e.tensor_copy(out=INTB, in_=mk.SCR4[:, 1024:2048])
                e.drain()
                vBB.inc(e.tensor_tensor(out=INTL, in0=mk.SCR4[:, 1024:2048],
                                        in1=INTB, op=ALU.subtract))
                e.drain()

        @block.scalar
        def _(e):
            for i in range(32):
                e.wait_ge(vB.h, vB.b + i + 1)
                if i >= 1:
                    e.wait_ge(vBB.h, vBB.b + i)
                sB.inc(e.activation(out=mk.SCR4[:, 1024:2048], in_=mk.SCR4[:, 0:1024],
                                    func=AF.Square, bias=0.0, scale=1.0))

        @block.gpsimd
        def _(e):
            for i in range(32):
                e.wait_ge(vBB.h, vBB.b + i + 1)
                INTB = mk.SCR2[:, (i % 2) * 1024:(i % 2) * 1024 + 1024]
                INTL = mk.SCR2[:, 2048 + (i % 2) * 1024:2048 + (i % 2) * 1024 + 1024]
                wint[i % 2].dinc(e.dma_start(
                    out=mk.ap("int1")[i * 128:(i + 1) * 128, :], in_=INTB))
                wint[i % 2].dinc(e.dma_start(
                    out=mk.ap("int1_lo")[i * 128:(i + 1) * 128, :], in_=INTL))

        self.emit_fwht_block(tag + "c", "int1", I1, 32, 127.0, -127.0, 127.0,
                             "u1", "v1", "qd1", wint)

        self.emit_down_block(tag + "d", "qd1", 32, "tw_l1dn", I1, 12, 0.5,
                             xin, xout)

    def ph_l0attn(self, xin, xout, tag):
        mk, block = self, self.block
        sv = self.sem("sv")
        scc = self.sem("scc")
        fence = self.snapshot()
        fx = [self.semv("fx" + str(j)) for j in range(2)]
        wqx = [self.semv("wqx" + str(j)) for j in range(2)]
        vA = self.semv("vA")
        sA = self.semv("sA")
        fqt = self.semv("fqt")
        fg = [self.semv("fg" + str(j)) for j in range(2)]
        tB = self.semv("tB")
        vB = self.semv("vB")
        sB = self.semv("sB")
        wkv = [self.semv("wkv" + ("" if j == 0 else "1")) for j in range(2)]
        fkc = [self.semv("fkc" + str(j)) for j in range(2)]
        vK = self.semv("vK")
        wkb = [self.semv("wkb" + ("" if j == 0 else "1")) for j in range(2)]
        fsc = [self.semv("fsc" + str(j)) for j in range(2)]
        tS = self.semv("tS")
        vS = self.semv("vS")
        sS = self.semv("sS")
        wpb = self.semv("wpb")
        fpt = [self.semv("fpt" + str(j)) for j in range(2)]
        fvg = [self.semv("fvg" + str(j)) for j in range(2)]
        tX = self.semv("tX")
        vX = self.semv("vX")
        fwo = [self.semv("fwo" + str(j)) for j in range(2)]
        fxr = [self.semv("fxr" + str(j)) for j in range(2)]
        tO = self.semv("tO")
        vO = self.semv("vO")
        wxo = [self.semv("wxo" + str(j)) for j in range(2)]
        fbi = self.semv("fbi")
        fbo = self.semv("fbo")
        tbo = self.semv("tbo")

        QS = mk.SLAB[:, 0:16384]
        CTX = mk.SLAB[:, 16384:32768]
        PTT = lambda s: mk.SLAB[:, 32768 + s * 1024:32768 + s * 1024 + 1024]
        VGT = lambda s: mk.SLAB[:, 34816 + s * 512:34816 + s * 512 + 512]
        CST = lambda s: mk.SLAB[:, 35840 + s * 2048:35840 + s * 2048 + 2048]
        STG = lambda s: mk.SCR4[:, s * 2048:s * 2048 + 2048]

        @block.sync
        def _(e):
            self.emit_fence(e, fence)
            fbi.dinc(e.dma_start(out=mk.VEC[:, 32:80], in_=mk.ap("b_qkvT")))
            for m in range(8):
                if m >= 2:
                    e.wait_ge(vA.h, vA.b + 3 * (m - 2) + 3)
                fx[m % 2].dinc(e.dma_start(
                    out=mk.SCR4[:, (m % 2) * 2048:(m % 2) * 2048 + 2048],
                    in_=mk.ap(xin)[m * 128:(m + 1) * 128, :]))

        @block.scalar
        def _(e):
            self.emit_fence(e, fence)
            for m in range(8):
                XA = mk.SCR4[:, (m % 2) * 2048:(m % 2) * 2048 + 2048]
                e.wait_ge(fx[m % 2].h, fx[m % 2].b + 16 * (m // 2 + 1))
                if m >= 1:
                    e.wait_ge(vA.h, vA.b + 3 * (m - 1) + 1)
                sA.inc(e.activation(out=mk.SCR4[:, 4096:6144], in_=XA,
                                    func=AF.Square, bias=0.0, scale=1.0,
                                    accum_out=mk.VEC[:, 3:4]))
                e.wait_ge(vA.h, vA.b + 3 * m + 1)
                sA.inc(e.activation(out=mk.VEC[:, 7:8], in_=mk.VEC[:, 6:7],
                                    func=AF.Sqrt, bias=0.0, scale=1.0))

        @block.vector
        def _(e):
            self.emit_fence(e, fence)
            for m in range(8):
                XA = mk.SCR4[:, (m % 2) * 2048:(m % 2) * 2048 + 2048]
                e.wait_ge(fx[m % 2].h, fx[m % 2].b + 16 * (m // 2 + 1))
                e.wait_ge(sA.h, sA.b + 2 * m + 1)
                self.emit_ln_stats(e, XA, 2048)
                vA.inc(self.emit_ln_small(e, 2048))
                e.wait_ge(sA.h, sA.b + 2 * m + 2)
                vA.inc(self.emit_ln_small2(e, 2048, None))
                if m >= 2:
                    e.wait_ge(wqx[m % 2].h, wqx[m % 2].b + 16 * (m // 2))
                XLNB = mk.SCR2[:, (m % 2) * 2048:(m % 2) * 2048 + 2048]
                vA.inc(e.tensor_scalar(out=XLNB, in0=XA, scalar1=mk.VEC[:, 8:9],
                                       scalar2=mk.VEC[:, 11:12],
                                       op0=ALU.mult, op1=ALU.add))
                e.drain()

        @block.gpsimd
        def _(e):
            self.emit_fence(e, fence)
            for m in range(8):
                e.wait_ge(vA.h, vA.b + 3 * m + 3)
                XLNB = mk.SCR2[:, (m % 2) * 2048:(m % 2) * 2048 + 2048]
                wqx[m % 2].dinc(e.dma_start(
                    out=mk.ap("qx0")[m * 128:(m + 1) * 128, :], in_=XLNB))

        @block.sync
        def _(e):
            e.wait_ge(wqx[0].h, wqx[0].n)
            e.wait_ge(wqx[1].h, wqx[1].n)
            for k in range(16):
                fqt.dinc(e.dma_start_transpose(
                    out=mk.SLAB[:, 16384 + k * 1024:16384 + (k + 1) * 1024],
                    in_=mk.ap("qx0")[:, k * 128:(k + 1) * 128]))
            for i in range(48):
                rb, c0 = (i * 128) // 768, (i * 128) % 768
                if i >= 2:
                    e.wait_ge(tB.h, tB.b + i - 1)
                srcw = mk.ap("tw_qkv").rearrange(
                    "(rb kp) c -> kp rb c", kp=2048)[:, rb, c0:c0 + 128]
                srcw = srcw.rearrange("(kc p) c -> p kc c", p=128)
                dstw = mk.WST[:, (i % 2) * 2048:(i % 2) * 2048 + 2048]
                dstw = dstw.rearrange("p (kc c) -> p kc c", c=128)
                fg[i % 2].dinc(e.dma_start(out=dstw, in_=srcw))

        XLNT = lambda k: mk.SLAB[:, 16384 + k * 1024:16384 + (k + 1) * 1024]

        @block.tensor
        def _(e):
            e.wait_ge(fqt.h, fqt.b + 16 * 16)
            for i in range(48):
                e.wait_ge(fg[i % 2].h, fg[i % 2].b + 16 * (i // 2 + 1))
                if i >= 1:
                    e.wait_ge(vB.h, vB.b + i)
                for k in range(16):
                    GW = mk.WST[:, (i % 2) * 2048 + k * 128:(i % 2) * 2048 + (k + 1) * 128]
                    e.matmul(mk.PA[:, 0:512], GW, XLNT(k)[:, 0:512],
                             start=(k == 0), stop=(k == 15))
                    ins = e.matmul(mk.PA[:, 512:1024], GW, XLNT(k)[:, 512:1024],
                                   start=(k == 0), stop=(k == 15))
                tB.inc(ins)

        @block.vector
        def _(e):
            e.wait_ge(fbi.h, fbi.b + 16)
            for i in range(48):
                e.wait_ge(tB.h, tB.b + i + 1)
                bcol = mk.VEC[:, 32 + i:33 + i]
                if i < 16:
                    vB.inc(e.tensor_scalar(out=QS[:, i * 1024:(i + 1) * 1024],
                                           in0=mk.PA[:, 0:1024], scalar1=bcol,
                                           scalar2=None, op0=ALU.add))
                else:
                    if i >= 18:
                        e.wait_ge(wkv[i % 2].h,
                                  wkv[i % 2].b + 16 * ((i - 18) // 2 + 1))
                    vB.inc(e.tensor_scalar(
                        out=mk.SCR4[:, 4096 + (i % 2) * 1024:4096 + (i % 2) * 1024 + 1024],
                        in0=mk.PA[:, 0:1024], scalar1=bcol,
                        scalar2=None, op0=ALU.add))
                e.drain()

        @block.gpsimd
        def _(e):
            for i in range(16, 48):
                e.wait_ge(vB.h, vB.b + i + 1)
                stg = mk.SCR4[:, 4096 + (i % 2) * 1024:4096 + (i % 2) * 1024 + 1024]
                if i < 32:
                    wkv[i % 2].dinc(e.dma_start(
                        out=mk.ap("kag_in")[(i - 16) * 128:(i - 15) * 128, :], in_=stg))
                else:
                    wkv[i % 2].dinc(e.dma_start(
                        out=mk.ap("vag_inF")[(i - 32) * 128:(i - 31) * 128, :], in_=stg))

        @block.gpsimd
        def _(e):
            e.wait_ge(wkv[0].h, wkv[0].b + 16 * 16)
            e.wait_ge(wkv[1].h, wkv[1].b + 16 * 16)
            scc.inc(e.collective_compute(
                "AllGather", ALU.bypass, replica_groups=[list(range(NCORE))],
                ins=[mk.ap("kag_in")], outs=[mk.ap("kag_out")]))
            scc.inc(e.collective_compute(
                "AllGather", ALU.bypass, replica_groups=[list(range(NCORE))],
                ins=[mk.ap("vag_inF")], outs=[mk.ap("vag_out2")]))
        cc_kv = scc.n

        @block.sync
        def _(e):
            e.wait_ge(scc.h, cc_kv)
            for j in range(256):
                if j >= 2:
                    e.wait_ge(vK.h, vK.b + j - 1)
                srcnm = "kag_out" if j < 128 else "vag_out2"
                r0 = (j % 128) * 128
                fkc[j % 2].dinc(e.dma_start(
                    out=mk.SCR4[:, (j % 2) * 1024:(j % 2) * 1024 + 1024],
                    in_=mk.ap(srcnm)[r0:r0 + 128, :]))

        @block.vector
        def _(e):
            for j in range(256):
                e.wait_ge(fkc[j % 2].h, fkc[j % 2].b + 16 * (j // 2 + 1))
                if j >= 2:
                    e.wait_ge(wkb[j % 2].h, wkb[j % 2].b + 16 * (j // 2))
                CSTT = mk.SCR2[:, (j % 2) * 1024:(j % 2) * 1024 + 1024]
                vK.inc(e.tensor_copy(
                    out=CSTT, in_=mk.SCR4[:, (j % 2) * 1024:(j % 2) * 1024 + 1024]))
                e.drain()

        @block.gpsimd
        def _(e):
            for j in range(256):
                e.wait_ge(vK.h, vK.b + j + 1)
                dstnm = "kgb" if j < 128 else "vgbF"
                r0 = (j % 128) * 128
                CSTT = mk.SCR2[:, (j % 2) * 1024:(j % 2) * 1024 + 1024]
                wkb[j % 2].dinc(e.dma_start(out=mk.ap(dstnm)[r0:r0 + 128, :], in_=CSTT))

        @block.sync
        def _(e):
            e.wait_ge(wkb[0].h, wkb[0].b + 16 * 128)
            e.wait_ge(wkb[1].h, wkb[1].b + 16 * 128)
            for q in range(8):
                for rb in range(8):
                    j = q * 8 + rb
                    if j >= 2:
                        e.wait_ge(tS.h, tS.b + j - 1)
                    src3 = mk.ap("kgb").rearrange("(rb h) t -> h rb t", h=H)
                    KT = mk.SLAB[:, 32768 + (j % 2) * 16384:32768 + (j % 2) * 16384 + 16384]
                    d3 = KT.rearrange("p (kc t) -> p kc t", t=1024)
                    s4 = src3[:, rb, :].rearrange("(kc p) t -> p kc t", p=128)
                    fsc[j % 2].dinc(e.dma_start(out=d3, in_=s4))

        @block.tensor
        def _(e):
            for q in range(8):
                for rb in range(8):
                    j = q * 8 + rb
                    e.wait_ge(fsc[j % 2].h, fsc[j % 2].b + 16 * (j // 2 + 1))
                    if j >= 1:
                        e.wait_ge(vS.h, vS.b + 10 * ((j - 1) // 8) + (j - 1) % 8 + 1)
                    KT = mk.SLAB[:, 32768 + (j % 2) * 16384:32768 + (j % 2) * 16384 + 16384]
                    for k in range(16):
                        e.matmul(mk.PA[:, 0:512],
                                 QS[:, k * 1024 + q * 128:k * 1024 + q * 128 + 128],
                                 KT[:, k * 1024:k * 1024 + 512],
                                 start=(k == 0), stop=(k == 15))
                        ins = e.matmul(mk.PA[:, 512:1024],
                                       QS[:, k * 1024 + q * 128:k * 1024 + q * 128 + 128],
                                       KT[:, k * 1024 + 512:(k + 1) * 1024],
                                       start=(k == 0), stop=(k == 15))
                    tS.inc(ins)

        @block.vector
        def _(e):
            for q in range(8):
                for rb in range(8):
                    j = q * 8 + rb
                    e.wait_ge(tS.h, tS.b + j + 1)
                    vS.inc(e.tensor_scalar(out=mk.SCR4[:, rb * 1024:(rb + 1) * 1024],
                                           in0=mk.PA[:, 0:1024], scalar1=RSQH,
                                           scalar2=None, op0=ALU.mult))
                    e.drain()
                e.tensor_reduce(out=mk.VEC[:, 20:21], in_=mk.SCR4[:, 0:8192],
                                axis=AX, op=ALU.max)
                e.drain()
                e.tensor_scalar(out=mk.VEC[:, 21:22], in0=mk.VEC[:, 20:21],
                                scalar1=-1.0, scalar2=None, op0=ALU.mult)
                e.drain()
                vS.inc(e.tensor_scalar(out=mk.SCR4[:, 0:8192], in0=mk.SCR4[:, 0:8192],
                                       scalar1=mk.VEC[:, 21:22], scalar2=None,
                                       op0=ALU.add))
                e.drain()
                e.wait_ge(sS.h, sS.b + q + 1)
                if q >= 1:
                    e.wait_ge(wpb.h, wpb.b + 16 * q)
                e.reciprocal(out=mk.VEC[:, 23:24], in_=mk.VEC[:, 22:23])
                e.drain()
                vS.inc(e.tensor_scalar(out=mk.SCR2[:, 0:8192], in0=mk.SCR2[:, 0:8192],
                                       scalar1=mk.VEC[:, 23:24], scalar2=None,
                                       op0=ALU.mult))
                e.drain()

        @block.scalar
        def _(e):
            for q in range(8):
                e.wait_ge(vS.h, vS.b + q * 10 + 9)
                if q >= 1:
                    e.wait_ge(wpb.h, wpb.b + 16 * q)
                sS.inc(e.activation(out=mk.SCR2[:, 0:8192], in_=mk.SCR4[:, 0:8192],
                                    func=AF.Exp, bias=0.0, scale=1.0,
                                    accum_out=mk.VEC[:, 22:23]))

        @block.gpsimd
        def _(e):
            for q in range(8):
                e.wait_ge(vS.h, vS.b + q * 10 + 10)
                wpb.dinc(e.dma_start(out=mk.ap("v0")[q * 128:(q + 1) * 128, :],
                                     in_=mk.SCR2[:, 0:8192]))

        HGRP = [(0, 3), (3, 3), (6, 3), (9, 3), (12, 2), (14, 2)]
        @block.sync
        def _(e):
            e.wait_ge(wpb.h, wpb.b + 16 * 8)
            e.wait_ge(wkb[0].h, wkb[0].b + 16 * 128)
            e.wait_ge(wkb[1].h, wkb[1].b + 16 * 128)
            for gi, (h0, nh) in enumerate(HGRP):
                for kc in range(64):
                    j = gi * 64 + kc
                    if j >= 2:
                        e.wait_ge(tX.h, tX.b + j - 1)
                    rb, tw = kc // 8, kc % 8
                    fvg[j % 2].dinc(e.dma_start_transpose(
                        out=VGT(j % 2)[:, 0:nh * 128],
                        in_=mk.ap("vgbF")[rb * H + h0 * 128:rb * H + (h0 + nh) * 128,
                                          tw * 128:(tw + 1) * 128]))
                    fpt[j % 2].dinc(e.dma_start_transpose(
                        out=PTT(j % 2),
                        in_=mk.ap("v0")[:, kc * 128:(kc + 1) * 128]))

        @block.tensor
        def _(e):
            PSUMS = [mk.PA, mk.PB, mk.PC]
            for gi, (h0, nh) in enumerate(HGRP):
                if gi >= 1:
                    e.wait_ge(vX.h, vX.b + sum(HGRP[g][1] for g in range(gi)))
                for kc in range(64):
                    j = gi * 64 + kc
                    e.wait_ge(fvg[j % 2].h, fvg[j % 2].b + 16 * (j // 2 + 1))
                    e.wait_ge(fpt[j % 2].h, fpt[j % 2].b + 16 * (j // 2 + 1))
                    for hh in range(nh):
                        VT_ = VGT(j % 2)[:, hh * 128:(hh + 1) * 128]
                        e.matmul(PSUMS[hh][:, 0:512], VT_, PTT(j % 2)[:, 0:512],
                                 start=(kc == 0), stop=(kc == 63))
                        ins = e.matmul(PSUMS[hh][:, 512:1024], VT_,
                                       PTT(j % 2)[:, 512:1024],
                                       start=(kc == 0), stop=(kc == 63))
                    tX.inc(ins)

        @block.vector
        def _(e):
            PSUMS = [mk.PA, mk.PB, mk.PC]
            for gi, (h0, nh) in enumerate(HGRP):
                e.wait_ge(tX.h, tX.b + (gi + 1) * 64)
                for hh in range(nh):
                    vX.inc(e.tensor_copy(
                        out=CTX[:, (h0 + hh) * 1024:(h0 + hh + 1) * 1024],
                        in_=PSUMS[hh][:, 0:1024]))
                    e.drain()

        @block.sync
        def _(e):
            e.wait_ge(wpb.h, wpb.b + 16 * 8)
            fbo.dinc(e.dma_start(out=mk.SCR4[0:1, 4096:6144], in_=mk.ap("b_out")))
            for m in range(8):
                if m >= 2:
                    e.wait_ge(wxo[m % 2].h, wxo[m % 2].b + 16 * ((m - 2) // 2 + 1))
                fxr[m % 2].dinc(e.dma_start(
                    out=mk.SCR4[:, (m % 2) * 2048:(m % 2) * 2048 + 2048],
                    in_=mk.ap(xin)[m * 128:(m + 1) * 128, :]))
                for k in range(16):
                    ki = m * 16 + k
                    if ki >= 2:
                        e.wait_ge(tO.h, tO.b + ki - 1)
                    srcw = mk.ap("tw_out0").rearrange(
                        "(rb j) c -> j rb c", j=H)[k * 128:(k + 1) * 128, :, :]
                    dstw = mk.WST[:, (ki % 2) * 2048:(ki % 2) * 2048 + 2048].rearrange(
                        "p (rb c) -> p rb c", c=256)
                    fwo[ki % 2].dinc(e.dma_start(out=dstw, in_=srcw))

        @block.tensor
        def _(e):
            e.wait_ge(fbo.h, fbo.b + 16)
            e.wait_ge(vX.h, vX.b + 16)
            for c in range(4):
                PS = mk.PA if c < 2 else mk.PB
                tbo.inc(e.matmul(PS[:, (c % 2) * 512:(c % 2) * 512 + 512],
                                 mk.ONESF[0:1, :],
                                 mk.SCR4[0:1, 4096 + c * 512:4096 + (c + 1) * 512],
                                 start=True, stop=True, skip_group_check=True))
            for m in range(8):
                if m == 0:
                    e.wait_ge(vO.h, vO.b + 1)
                for k in range(16):
                    ki = m * 16 + k
                    e.wait_ge(fwo[ki % 2].h, fwo[ki % 2].b + 16 * (ki // 2 + 1))
                    if k == 0 and m >= 1:
                        e.wait_ge(vO.h, vO.b + m + 1)
                    QK = CTX[:, k * 1024 + m * 128:k * 1024 + m * 128 + 128]
                    for rb in range(8):
                        PS = mk.PA if rb < 4 else mk.PB
                        ins = e.matmul(
                            PS[:, (rb % 4) * 256:(rb % 4) * 256 + 256], QK,
                            mk.WST[:, (ki % 2) * 2048 + rb * 256:(ki % 2) * 2048 + (rb + 1) * 256],
                            start=(k == 0 and rb % 2 == 0), stop=(k == 15),
                            skip_group_check=True)
                    tO.inc(ins)

        @block.vector
        def _(e):
            e.wait_ge(tbo.h, tbo.b + 4)
            e.tensor_copy(out=mk.SCR4[:, 6144:7168], in_=mk.PA[:, 0:1024])
            e.drain()
            vO.inc(e.tensor_copy(out=mk.SCR4[:, 7168:8192], in_=mk.PB[:, 0:1024]))
            e.drain()
            for m in range(8):
                e.wait_ge(tO.h, tO.b + (m + 1) * 16)
                e.wait_ge(fxr[m % 2].h, fxr[m % 2].b + 16 * (m // 2 + 1))
                XIN = mk.SCR4[:, (m % 2) * 2048:(m % 2) * 2048 + 2048]
                for half in range(2):
                    PS = mk.PA if half == 0 else mk.PB
                    BOR = mk.SCR4[:, 6144 + half * 1024:6144 + (half + 1) * 1024]
                    e.tensor_tensor(out=XIN[:, half * 1024:(half + 1) * 1024],
                                    in0=XIN[:, half * 1024:(half + 1) * 1024],
                                    in1=BOR, op=ALU.add)
                    e.drain()
                    ins = e.tensor_tensor(out=XIN[:, half * 1024:(half + 1) * 1024],
                                          in0=XIN[:, half * 1024:(half + 1) * 1024],
                                          in1=PS[:, 0:1024], op=ALU.add)
                    e.drain()
                vO.inc(ins)

        @block.gpsimd
        def _(e):
            for m in range(8):
                e.wait_ge(vO.h, vO.b + 2 + m)
                XIN = mk.SCR4[:, (m % 2) * 2048:(m % 2) * 2048 + 2048]
                wxo[m % 2].dinc(e.dma_start(
                    out=mk.ap(xout)[m * 128:(m + 1) * 128, :], in_=XIN))

    def ph_l1attn(self, xin, xout, tag):
        mk, block = self, self.block
        scc = self.sem("scc")
        fence = self.snapshot()
        fx = [self.semv("fx" + str(j)) for j in range(2)]
        wxa = self.semv("wqx0")
        vA = self.semv("vA")
        sA = self.semv("sA")
        fxc = [self.semv("fkc" + str(j)) for j in range(2)]
        vXc = self.semv("vK")
        wxb = [self.semv("wkb" + ("" if j == 0 else "1")) for j in range(2)]
        fwin = self.semv("fqt")
        fxg = self.semv("fqf")
        tC = self.semv("tB")
        evv = self.semv("vB")
        evs = self.semv("sB")
        wgs = [self.semv("wkv" + ("" if j == 0 else "1")) for j in range(2)]
        fcv = self.semv("fbi")
        vcv = self.semv("vBB")
        scv = self.semv("sS")
        fxp = self.semv("fbo")
        vxp = self.semv("vQ")
        tXp = self.semv("tbo")
        vxs = self.semv("vS")
        wxp = self.semv("wpb")
        fsc2 = self.semv("fw0")
        fdt = self.semv("fw1")
        falog = self.semv("fvt0")
        fss = self.semv("fu0")
        fgt = self.semv("fu1")
        a5V = self.semv("vC1")
        a5S = self.semv("sQ")
        a5tE = self.semv("tC1")
        wym = self.semv("wint0")
        fys = self.semv("fvt1")
        fwo2 = [self.semv("fwd" + str(j)) for j in range(2)]
        vwo = self.semv("vC2")
        tH = self.semv("tC2")
        vev = self.semv("vD")
        wop = [self.semv("wint1"), self.semv("wop1")]
        fri = [self.semv("fic" + str(j)) for j in range(2)]
        fxi = [self.semv("fxr" + str(j)) for j in range(2)]
        vri = self.semv("vX")
        wxo2 = [self.semv("wxo" + str(j)) for j in range(2)]

        XC = lambda dj: mk.SLAB[:, dj * 8192:(dj + 1) * 8192]
        HSX = lambda ht: mk.SLAB[:, 32768 + ht * 8192:32768 + (ht + 1) * 8192]

        VPB, SPB, TPB = 43, 19, 51
        VPD, SPD = VPB * 8 + 1, SPB * 8 + 1
        Vb = lambda dj, tc: 1 + VPD * dj + 1 + VPB * tc
        Sb = lambda dj, tc: SPD * dj + 1 + SPB * tc
        Tb = lambda dj, tc: TPB * (dj * 8 + tc)

        @block.sync
        def _(e):
            self.emit_fence(e, fence)
            for m in range(8):
                if m >= 2:
                    e.wait_ge(vA.h, vA.b + 3 * (m - 2) + 3)
                fx[m % 2].dinc(e.dma_start(
                    out=mk.SCR4[:, (m % 2) * 2048:(m % 2) * 2048 + 2048],
                    in_=mk.ap(xin)[m * 128:(m + 1) * 128, :]))

        @block.scalar
        def _(e):
            self.emit_fence(e, fence)
            for m in range(8):
                XA = mk.SCR4[:, (m % 2) * 2048:(m % 2) * 2048 + 2048]
                e.wait_ge(fx[m % 2].h, fx[m % 2].b + 16 * (m // 2 + 1))
                if m >= 1:
                    e.wait_ge(vA.h, vA.b + 3 * (m - 1) + 1)
                sA.inc(e.activation(out=mk.SCR4[:, 6144:8192], in_=XA,
                                    func=AF.Square, bias=0.0, scale=1.0,
                                    accum_out=mk.VEC[:, 3:4]))
                e.wait_ge(vA.h, vA.b + 3 * m + 1)
                sA.inc(e.activation(out=mk.VEC[:, 7:8], in_=mk.VEC[:, 6:7],
                                    func=AF.Sqrt, bias=0.0, scale=1.0))

        @block.vector
        def _(e):
            self.emit_fence(e, fence)
            for m in range(8):
                XA = mk.SCR4[:, (m % 2) * 2048:(m % 2) * 2048 + 2048]
                e.wait_ge(fx[m % 2].h, fx[m % 2].b + 16 * (m // 2 + 1))
                e.wait_ge(sA.h, sA.b + 2 * m + 1)
                self.emit_ln_stats(e, XA, 2048)
                vA.inc(self.emit_ln_small(e, 2048))
                e.wait_ge(sA.h, sA.b + 2 * m + 2)
                vA.inc(self.emit_ln_small2(e, 2048, None))
                if m >= 1:
                    e.wait_ge(wxa.h, wxa.b + 16 * m)
                vA.inc(e.tensor_scalar(out=mk.SCR4[:, 4096:6144], in0=XA,
                                       scalar1=mk.VEC[:, 8:9],
                                       scalar2=mk.VEC[:, 11:12],
                                       op0=ALU.mult, op1=ALU.add))
                e.drain()

        @block.gpsimd
        def _(e):
            self.emit_fence(e, fence)
            for m in range(8):
                e.wait_ge(vA.h, vA.b + 3 * m + 3)
                wxa.dinc(e.dma_start(
                    out=mk.ap("xag_in")[m * 128:(m + 1) * 128, :],
                    in_=mk.SCR4[:, 4096:6144]))

        @block.gpsimd
        def _(e):
            e.wait_ge(wxa.h, wxa.b + 16 * 8)
            scc.inc(e.collective_compute(
                "AllGather", ALU.bypass, replica_groups=[list(range(NCORE))],
                ins=[mk.ap("xag_in")], outs=[mk.ap("xag_out")]))
        cc_xa = scc.n

        @block.sync
        def _(e):
            e.wait_ge(scc.h, cc_xa)
            for j in range(64):
                if j >= 2:
                    e.wait_ge(vXc.h, vXc.b + j - 1)
                fxc[j % 2].dinc(e.dma_start(
                    out=mk.SCR4[:, (j % 2) * 2048:(j % 2) * 2048 + 2048],
                    in_=mk.ap("xag_out")[j * 128:(j + 1) * 128, :]))

        @block.vector
        def _(e):
            for j in range(64):
                e.wait_ge(fxc[j % 2].h, fxc[j % 2].b + 16 * (j // 2 + 1))
                if j >= 2:
                    e.wait_ge(wxb[j % 2].h, wxb[j % 2].b + 16 * (j // 2))
                vXc.inc(e.tensor_copy(
                    out=mk.SCR2[:, (j % 2) * 2048:(j % 2) * 2048 + 2048],
                    in_=mk.SCR4[:, (j % 2) * 2048:(j % 2) * 2048 + 2048]))
                e.drain()

        @block.gpsimd
        def _(e):
            for j in range(64):
                e.wait_ge(vXc.h, vXc.b + j + 1)
                wxb[j % 2].dinc(e.dma_start(
                    out=mk.ap("xgb")[j * 128:(j + 1) * 128, :],
                    in_=mk.SCR2[:, (j % 2) * 2048:(j % 2) * 2048 + 2048]))

        evv_c, evs_c = [], []
        vv = ss = 0
        for gi in range(64):
            if gi % 8 < 4:
                vv += 1
            else:
                ss += 1
            evv_c.append(vv)
            evs_c.append(ss)

        @block.sync
        def _(e):
            e.wait_ge(wxb[0].h, wxb[0].b + 16 * 32)
            e.wait_ge(wxb[1].h, wxb[1].b + 16 * 32)
            dstw = mk.SLAB[:, 16384:32768].rearrange("p (kc c) -> p kc c", c=1024)
            srcw = mk.ap("tsl_inp").rearrange("(kc p) c -> p kc c", p=128)
            fwin.dinc(e.dma_start(out=dstw, in_=srcw))
            for tt in range(8):
                if tt >= 1:
                    e.wait_ge(tC.h, tC.b + 8 * tt)
                for k in range(16):
                    fxg.dinc(e.dma_start_transpose(
                        out=mk.SLAB[:, k * 1024:(k + 1) * 1024],
                        in_=mk.ap("xgb")[tt * 1024:(tt + 1) * 1024,
                                         k * 128:(k + 1) * 128]))

        @block.tensor
        def _(e):
            e.wait_ge(fwin.h, fwin.b + 16)
            for tt in range(8):
                e.wait_ge(fxg.h, fxg.b + 16 * 16 * (tt + 1))
                for ot in range(8):
                    gi = tt * 8 + ot
                    PS = mk.PA if ot % 2 == 0 else mk.PB
                    if gi >= 2:
                        pot = (gi - 2) % 8
                        if pot < 4:
                            e.wait_ge(evv.h, evv.b + evv_c[gi - 2])
                        else:
                            e.wait_ge(evs.h, evs.b + evs_c[gi - 2])
                    for k in range(16):
                        LW = mk.SLAB[:, 16384 + k * 1024 + ot * 128:
                                     16384 + k * 1024 + (ot + 1) * 128]
                        XG = mk.SLAB[:, k * 1024:(k + 1) * 1024]
                        e.matmul(PS[:, 0:512], LW, XG[:, 0:512],
                                 start=(k == 0), stop=(k == 15))
                        ins = e.matmul(PS[:, 512:1024], LW, XG[:, 512:1024],
                                       start=(k == 0), stop=(k == 15))
                    tC.inc(ins)

        @block.vector
        def _(e):
            for tt in range(8):
                for ot in range(4):
                    gi = tt * 8 + ot
                    e.wait_ge(tC.h, tC.b + gi + 1)
                    PS = mk.PA if ot % 2 == 0 else mk.PB
                    evv.inc(e.tensor_copy(
                        out=HSX(ot)[:, tt * 1024:(tt + 1) * 1024],
                        in_=PS[:, 0:1024]))
                    e.drain()

        @block.scalar
        def _(e):
            for tt in range(8):
                for ot in range(4, 8):
                    gi = tt * 8 + ot
                    gs = tt * 4 + (ot - 4)
                    e.wait_ge(tC.h, tC.b + gi + 1)
                    if gs >= 2:
                        e.wait_ge(wgs[gs % 2].h, wgs[gs % 2].b + 16 * (gs // 2))
                    PS = mk.PA if ot % 2 == 0 else mk.PB
                    evs.inc(e.activation(
                        out=mk.SCR2[:, (gs % 2) * 1024:(gs % 2) * 1024 + 1024],
                        in_=PS[:, 0:1024], func=AF.Silu, bias=0.0, scale=1.0))

        @block.gpsimd
        def _(e):
            for tt in range(8):
                for ot in range(4, 8):
                    gs = tt * 4 + (ot - 4)
                    e.wait_ge(evs.h, evs.b + gs + 1)
                    wgs[gs % 2].dinc(e.dma_start(
                        out=mk.ap("gate_sil")[(ot - 4) * 128:(ot - 3) * 128,
                                              tt * 1024:(tt + 1) * 1024],
                        in_=mk.SCR2[:, (gs % 2) * 1024:(gs % 2) * 1024 + 1024]))

        @block.sync
        def _(e):
            for nm, dst in (("cw_slT", mk.VEC[:, 32:48]), ("cb_slT", mk.VEC[:, 48:52]),
                            ("dtb_slT", mk.VEC[:, 24:28]), ("dv_slT", mk.VEC[:, 28:32])):
                fcv.dinc(e.dma_start(out=dst, in_=mk.ap(nm)))

        @block.vector
        def _(e):
            e.wait_ge(evv.h, evv.b + 32)
            e.wait_ge(fcv.h, fcv.b + 64)
            for ht in range(4):
                if ht >= 1:
                    e.wait_ge(scv.h, scv.b + ht)
                ACC = mk.SCR4[:, 0:8192]
                e.tensor_scalar(out=ACC, in0=HSX(ht),
                                scalar1=mk.VEC[:, 32 + ht * 4 + 3:33 + ht * 4 + 3],
                                scalar2=None, op0=ALU.mult)
                e.drain()
                for kk in (2, 1, 0):
                    sh = 3 - kk
                    e.scalar_tensor_tensor(
                        out=ACC[:, sh:8192], in0=HSX(ht)[:, 0:8192 - sh],
                        scalar=mk.VEC[:, 32 + ht * 4 + kk:33 + ht * 4 + kk],
                        in1=ACC[:, sh:8192], op0=ALU.mult, op1=ALU.add)
                    e.drain()
                vcv.inc(e.tensor_scalar(out=ACC, in0=ACC,
                                        scalar1=mk.VEC[:, 48 + ht:49 + ht],
                                        scalar2=None, op0=ALU.add))
                e.drain()

        @block.scalar
        def _(e):
            for ht in range(4):
                e.wait_ge(vcv.h, vcv.b + ht + 1)
                scv.inc(e.activation(out=XC(ht), in_=mk.SCR4[:, 0:8192],
                                     func=AF.Silu, bias=0.0, scale=1.0))

        @block.sync
        def _(e):
            e.wait_ge(scv.h, scv.b + 4)
            e.wait_ge(wgs[0].h, wgs[0].b + 16 * 16)
            e.wait_ge(wgs[1].h, wgs[1].b + 16 * 16)
            for dk in range(4):
                fxp.dinc(e.dma_start(
                    out=mk.SCR4[:, dk * 160:(dk + 1) * 160],
                    in_=mk.ap("xpw_slT")[dk * 128:(dk + 1) * 128, :]))

        @block.vector
        def _(e):
            e.wait_ge(fxp.h, fxp.b + 16 * 4)
            vxp.inc(e.tensor_copy(out=mk.SCR2[:, 0:640], in_=mk.SCR4[:, 0:640]))
            e.drain()

        @block.tensor
        def _(e):
            e.wait_ge(vxp.h, vxp.b + 1)
            for tc in range(8):
                if tc >= 1:
                    e.wait_ge(vxs.h, vxs.b + 2 * tc)
                for dk in range(4):
                    L1 = mk.SCR2[:, dk * 160:dk * 160 + 128]
                    L2 = mk.SCR2[:, dk * 160 + 128:dk * 160 + 160]
                    RH = XC(dk)[:, tc * 1024:(tc + 1) * 1024]
                    e.matmul(mk.PA[:, 0:512], L1, RH[:, 0:512],
                             start=(dk == 0), stop=(dk == 3))
                    e.matmul(mk.PA[:, 512:1024], L1, RH[:, 512:1024],
                             start=(dk == 0), stop=(dk == 3))
                    e.matmul(mk.PB[0:32, 0:512], L2, RH[:, 0:512],
                             start=(dk == 0), stop=(dk == 3))
                    ins = e.matmul(mk.PB[0:32, 512:1024], L2, RH[:, 512:1024],
                                   start=(dk == 0), stop=(dk == 3))
                tXp.inc(ins)

        @block.vector
        def _(e):
            for tc in range(8):
                e.wait_ge(tXp.h, tXp.b + tc + 1)
                if tc >= 1:
                    e.wait_ge(wxp.h, wxp.b + 32 * tc)
                vxs.inc(e.tensor_copy(out=mk.SCR4[:, 1024:2048], in_=mk.PA[:, 0:1024]))
                e.drain()
                vxs.inc(e.tensor_copy(out=mk.SCR4[0:32, 2048:3072],
                                      in_=mk.PB[0:32, 0:1024]))
                e.drain()

        @block.gpsimd
        def _(e):
            for tc in range(8):
                e.wait_ge(vxs.h, vxs.b + 2 * tc + 1)
                wxp.dinc(e.dma_start(out=mk.ap("ssm_in")[tc, 0:128, :],
                                     in_=mk.SCR4[:, 1024:2048]))
                e.wait_ge(vxs.h, vxs.b + 2 * tc + 2)
                wxp.dinc(e.dma_start(out=mk.ap("ssm_in")[tc, 128:160, :],
                                     in_=mk.SCR4[0:32, 2048:3072]))
            e.wait_ge(wxp.h, wxp.b + 16 * 16)
            scc.inc(e.collective_compute(
                "AllReduce", ALU.add, replica_groups=[list(range(NCORE))],
                ins=[mk.ap("ssm_in")], outs=[mk.ap("ssm_out")]))
        cc_ssm = scc.n

        @block.sync
        def _(e):
            e.wait_ge(scc.h, cc_ssm)
            fsc2.dinc(e.dma_start(out=mk.WST[:, 0:2048], in_=mk.ap("c_nredf")))
            fsc2.dinc(e.dma_start(out=mk.WST[:, 6144:8192], in_=mk.ap("c_dmask2")))
            fdt.dinc(e.dma_start(out=mk.SCR4[:, 0:512], in_=mk.ap("dtw_slT")))
            for dj in range(4):
                if dj >= 1:
                    e.wait_ge(a5V.h, a5V.b + 1 + VPD * dj)
                falog.dinc(e.dma_start(
                    out=mk.SCR4[:, 2048:4096],
                    in_=mk.ap("alog_slE")[dj * 128:(dj + 1) * 128, :]))
                for tc in range(8):
                    B = dj * 8 + tc
                    vb = Vb(dj, tc)
                    if B >= 1:
                        e.wait_ge(a5V.h, a5V.b + vb - VPB + 3)
                    fss.dinc(e.dma_start(out=mk.SCR4[0:16, 4096:5120],
                                         in_=mk.ap("ssm_out")[tc, 128:144, :]))
                    fss.dinc(e.dma_start(out=mk.SCR4[0:16, 5120:6144],
                                         in_=mk.ap("ssm_out")[tc, 144:160, :]))
                    fss.dinc(e.dma_start(out=mk.SCR4[:, 6144:7168],
                                         in_=mk.ap("ssm_out")[tc, 0:128, :]))
                    if B >= 1:
                        e.wait_ge(a5V.h, a5V.b + vb)
                    fgt.dinc(e.dma_start(
                        out=mk.SCR2[:, 1024:2048],
                        in_=mk.ap("gate_sil")[dj * 128:(dj + 1) * 128,
                                              tc * 1024:(tc + 1) * 1024]))

        BREP = mk.SLAB[:, 32768:33792]
        CREP = mk.SLAB[:, 33792:34816]

        @block.tensor
        def _(e):
            e.wait_ge(fsc2.h, fsc2.b + 32)
            for dj in range(4):
                for tc in range(8):
                    B = dj * 8 + tc
                    vb, sb, tb = Vb(dj, tc), Sb(dj, tc), Tb(dj, tc)
                    e.wait_ge(a5V.h, a5V.b + vb + 3)
                    e.wait_ge(a5S.h, a5S.b + sb)
                    e.matmul(mk.PA[:, 0:512], mk.REPL16[:],
                             mk.SCR2[0:16, 3072:3584], start=True, stop=True)
                    a5tE.inc(e.matmul(mk.PA[:, 512:1024], mk.REPL16[:],
                                      mk.SCR2[0:16, 3584:4096],
                                      start=True, stop=True))
                    e.wait_ge(a5V.h, a5V.b + vb + 4)
                    e.matmul(mk.PA[:, 0:512], mk.REPL16[:],
                             mk.SCR2[0:16, 4096:4608], start=True, stop=True)
                    a5tE.inc(e.matmul(mk.PA[:, 512:1024], mk.REPL16[:],
                                      mk.SCR2[0:16, 4608:5120],
                                      start=True, stop=True))
                    e.wait_ge(a5V.h, a5V.b + vb + 5)
                    DTWT = mk.WST[:, 3088 + dj * 128:3088 + (dj + 1) * 128]
                    e.matmul(mk.PA[:, 0:512], DTWT, mk.SCR2[:, 5120:5632],
                             start=True, stop=True)
                    a5tE.inc(e.matmul(mk.PA[:, 512:1024], DTWT,
                                      mk.SCR2[:, 5632:6144],
                                      start=True, stop=True))
                    for jj in range(16):
                        if jj == 0:
                            e.wait_ge(a5V.h, a5V.b + vb + 9)
                        else:
                            e.wait_ge(a5S.h, a5S.b + sb + 3 + jj)
                            e.wait_ge(a5V.h, a5V.b + vb + 9 + 2 * (jj - 1) + 1)
                        ALH = mk.WST[:, 4096 + jj * 128:4096 + (jj + 1) * 128]
                        e.matmul(mk.PA[:, 0:512], ALH, mk.SCR2[:, 6144:6656],
                                 start=True, stop=True)
                        a5tE.inc(e.matmul(mk.PA[:, 512:1024], ALH,
                                          mk.SCR2[:, 6656:7168],
                                          start=True, stop=True))
                        DMS = mk.WST[:, 6144 + jj * 128:6144 + (jj + 1) * 128]
                        e.matmul(mk.PB[:, 0:512], DMS, mk.SCR2[:, 0:512],
                                 start=True, stop=True)
                        a5tE.inc(e.matmul(mk.PB[:, 512:1024], DMS,
                                          mk.SCR2[:, 512:1024],
                                          start=True, stop=True))
                        e.wait_ge(a5V.h, a5V.b + vb + 9 + 2 * jj + 2)
                        NRF = mk.WST[:, jj * 128:(jj + 1) * 128]
                        e.matmul(mk.PC[:, 0:512], NRF, mk.WST[:, 2048:2560],
                                 start=(jj == 0), stop=(jj == 15))
                        a5tE.inc(e.matmul(mk.PC[:, 512:1024], NRF,
                                          mk.WST[:, 2560:3072],
                                          start=(jj == 0), stop=(jj == 15)))

        @block.scalar
        def _(e):
            for dj in range(4):
                e.wait_ge(falog.h, falog.b + 16 * (dj + 1))
                e.wait_ge(a5V.h, a5V.b + 1 + VPD * dj)
                a5S.inc(e.activation(out=mk.SCR4[:, 0:2048],
                                     in_=mk.SCR4[:, 2048:4096],
                                     func=AF.Exp, bias=0.0, scale=1.0))
                for tc in range(8):
                    B = dj * 8 + tc
                    vb, sb, tb = Vb(dj, tc), Sb(dj, tc), Tb(dj, tc)
                    e.wait_ge(a5V.h, a5V.b + vb + 6)
                    a5S.inc(e.activation(out=mk.SCR4[:, 2048:3072],
                                         in_=mk.SCR4[:, 7168:8192],
                                         func=AF.Abs, bias=0.0, scale=1.0))
                    e.drain()
                    a5S.inc(e.activation(out=mk.SCR4[:, 0:1024],
                                         in_=mk.SCR4[:, 2048:3072],
                                         func=AF.Exp, bias=0.0, scale=-1.0))
                    e.drain()
                    a5S.inc(e.activation(out=mk.SCR4[:, 2048:3072],
                                         in_=mk.SCR4[:, 0:1024],
                                         func=AF.Ln, bias=1.0, scale=1.0))
                    e.drain()
                    for jj in range(16):
                        e.wait_ge(a5tE.h, a5tE.b + tb + 3 + 3 * jj + 1)
                        if jj >= 1:
                            e.wait_ge(a5V.h, a5V.b + vb + 9 + 2 * (jj - 1) + 2)
                        a5S.inc(e.activation(out=mk.SCR4[:, 0:1024],
                                             in_=mk.PA[:, 0:1024],
                                             func=AF.Exp, bias=0.0, scale=1.0))

        @block.vector
        def _(e):
            e.wait_ge(fdt.h, fdt.b + 16)
            e.wait_ge(fsc2.h, fsc2.b + 32)
            a5V.inc(e.tensor_copy(out=mk.WST[:, 3088:3600], in_=mk.SCR4[:, 0:512]))
            e.drain()
            e.memset(mk.HST[:], 0.0)
            e.drain()
            for dj in range(4):
                e.wait_ge(a5S.h, a5S.b + SPD * dj + 1)
                if dj >= 1:
                    e.wait_ge(a5tE.h, a5tE.b + TPB * 8 * dj - 2)
                a5V.inc(e.scalar_tensor_tensor(
                    out=mk.WST[:, 4096:6144], in0=mk.SCR4[:, 0:2048],
                    scalar=-1.0, in1=mk.WST[:, 6144:8192],
                    op0=ALU.mult, op1=ALU.mult))
                e.drain()
                for tc in range(8):
                    B = dj * 8 + tc
                    vb, sb, tb = Vb(dj, tc), Sb(dj, tc), Tb(dj, tc)
                    e.wait_ge(fss.h, fss.b + 16 * 3 * (B + 1))
                    if B >= 1:
                        e.wait_ge(a5tE.h, a5tE.b + tb - TPB + 3)
                    a5V.inc(e.tensor_copy(out=mk.SCR2[0:16, 3072:4096],
                                          in_=mk.SCR4[0:16, 4096:5120]))
                    e.drain()
                    a5V.inc(e.tensor_copy(out=mk.SCR2[0:16, 4096:5120],
                                          in_=mk.SCR4[0:16, 5120:6144]))
                    e.drain()
                    a5V.inc(e.tensor_copy(out=mk.SCR2[:, 5120:6144],
                                          in_=mk.SCR4[:, 6144:7168]))
                    e.drain()
                    e.wait_ge(a5tE.h, a5tE.b + tb + 1)
                    a5V.inc(e.tensor_copy(out=BREP, in_=mk.PA[:, 0:1024]))
                    e.drain()
                    e.wait_ge(a5tE.h, a5tE.b + tb + 2)
                    a5V.inc(e.tensor_copy(out=CREP, in_=mk.PA[:, 0:1024]))
                    e.drain()
                    e.wait_ge(a5tE.h, a5tE.b + tb + 3)
                    a5V.inc(e.tensor_scalar(out=mk.SCR4[:, 7168:8192],
                                            in0=mk.PA[:, 0:1024],
                                            scalar1=mk.VEC[:, 24 + dj:25 + dj],
                                            scalar2=None, op0=ALU.add))
                    e.drain()
                    a5V.inc(e.tensor_scalar(out=mk.SCR4[:, 1024:2048],
                                            in0=mk.SCR4[:, 7168:8192],
                                            scalar1=0.0, scalar2=None,
                                            op0=ALU.max))
                    e.drain()
                    e.wait_ge(a5S.h, a5S.b + sb + 3)
                    if B >= 1:
                        e.wait_ge(a5tE.h, a5tE.b + tb - 2)
                    a5V.inc(e.tensor_tensor(out=mk.SCR2[:, 6144:7168],
                                            in0=mk.SCR4[:, 1024:2048],
                                            in1=mk.SCR4[:, 2048:3072],
                                            op=ALU.add))
                    e.drain()
                    if B >= 1:
                        e.wait_ge(a5tE.h, a5tE.b + tb - 1)
                    a5V.inc(e.tensor_tensor(out=mk.SCR2[:, 0:1024],
                                            in0=mk.SCR2[:, 6144:7168],
                                            in1=XC(dj)[:, tc * 1024:(tc + 1) * 1024],
                                            op=ALU.mult))
                    e.drain()
                    for jj in range(16):
                        j64 = dj * 16 + jj
                        e.wait_ge(a5tE.h, a5tE.b + tb + 3 + 3 * jj + 2)
                        a5V.inc(e.tensor_tensor(out=mk.SCR4[:, 1024:2048],
                                                in0=mk.PB[:, 0:1024],
                                                in1=BREP,
                                                op=ALU.mult))
                        e.drain()
                        e.wait_ge(a5S.h, a5S.b + sb + 3 + jj + 1)
                        e.tensor_tensor_scan(out=mk.SCR4[:, 2048:3072],
                                             data0=mk.SCR4[:, 0:1024],
                                             data1=mk.SCR4[:, 1024:2048],
                                             initial=mk.HST[:, j64:j64 + 1],
                                             op0=ALU.mult, op1=ALU.add)
                        e.drain()
                        e.tensor_copy(out=mk.HST[:, j64:j64 + 1],
                                      in_=mk.SCR4[:, 3071:3072])
                        e.drain()
                        a5V.inc(e.tensor_tensor(out=mk.WST[:, 2048:3072],
                                                in0=mk.SCR4[:, 2048:3072],
                                                in1=CREP,
                                                op=ALU.mult))
                        e.drain()
                    e.wait_ge(a5tE.h, a5tE.b + tb + TPB)
                    a5V.inc(e.scalar_tensor_tensor(
                        out=mk.SCR4[:, 3072:4096],
                        in0=XC(dj)[:, tc * 1024:(tc + 1) * 1024],
                        scalar=mk.VEC[:, 28 + dj:29 + dj],
                        in1=mk.PC[:, 0:1024], op0=ALU.mult, op1=ALU.add))
                    e.drain()
                    e.wait_ge(fgt.h, fgt.b + 16 * (B + 1))
                    if B >= 1:
                        e.wait_ge(wym.h, wym.b + 16 * B)
                    a5V.inc(e.tensor_tensor(out=mk.SCR2[:, 2048:3072],
                                            in0=mk.SCR4[:, 3072:4096],
                                            in1=mk.SCR2[:, 1024:2048],
                                            op=ALU.mult))
                    e.drain()

        @block.gpsimd
        def _(e):
            for dj in range(4):
                for tc in range(8):
                    B = dj * 8 + tc
                    e.wait_ge(a5V.h, a5V.b + Vb(dj, tc) + VPB)
                    wym.dinc(e.dma_start(
                        out=mk.ap("ymd")[dj * 128:(dj + 1) * 128,
                                         tc * 1024:(tc + 1) * 1024],
                        in_=mk.SCR2[:, 2048:3072]))

        @block.sync
        def _(e):
            e.wait_ge(wym.h, wym.b + 16 * 32)
            dsty = mk.SLAB[:, 0:32768].rearrange("p (dk t) -> p dk t", t=8192)
            srcy = mk.ap("ymd").rearrange("(dk p) t -> p dk t", p=128)
            fys.dinc(e.dma_start(out=dsty, in_=srcy))
            for st in range(4):
                if st >= 2:
                    e.wait_ge(vwo.h, vwo.b + st - 1)
                fwo2[st % 2].dinc(e.dma_start(
                    out=mk.SCR4[:, (st % 2) * 2048:(st % 2) * 2048 + 2048],
                    in_=mk.ap("wsl_outp")[st * 128:(st + 1) * 128, :]))

        @block.vector
        def _(e):
            for st in range(4):
                e.wait_ge(fwo2[st % 2].h, fwo2[st % 2].b + 16 * (st // 2 + 1))
                vwo.inc(e.tensor_copy(
                    out=mk.WST[:, st * 2048:(st + 1) * 2048],
                    in_=mk.SCR4[:, (st % 2) * 2048:(st % 2) * 2048 + 2048]))
                e.drain()

        @block.tensor
        def _(e):
            e.wait_ge(fys.h, fys.b + 16)
            e.wait_ge(vwo.h, vwo.b + 4)
            for mt in range(64):
                if mt >= 1:
                    e.wait_ge(vev.h, vev.b + 2 * mt)
                for dk in range(4):
                    LY = mk.SLAB[:, dk * 8192 + mt * 128:dk * 8192 + (mt + 1) * 128]
                    for nb in range(4):
                        PS = mk.PA if nb < 2 else mk.PB
                        ins = e.matmul(
                            PS[:, (nb % 2) * 512:(nb % 2) * 512 + 512], LY,
                            mk.WST[:, dk * 2048 + nb * 512:dk * 2048 + (nb + 1) * 512],
                            start=(dk == 0), stop=(dk == 3))
                tH.inc(ins)

        @block.vector
        def _(e):
            for mt in range(64):
                e.wait_ge(tH.h, tH.b + mt + 1)
                if mt >= 2:
                    e.wait_ge(wop[mt % 2].h, wop[mt % 2].b + 16 * (mt // 2))
                SL = mk.SCR4[:, (mt % 2) * 2048:(mt % 2) * 2048 + 2048]
                e.tensor_copy(out=SL[:, 0:1024], in_=mk.PA[:, 0:1024])
                e.drain()
                vev.inc(e.tensor_copy(out=SL[:, 1024:2048], in_=mk.PB[:, 0:1024]))
                e.drain()
                vev.inc(e.engine_nop()) if False else None
                vev.inc(e.tensor_scalar(out=mk.VEC[:, 20:21], in0=mk.VEC[:, 20:21],
                                        scalar1=1.0, scalar2=None, op0=ALU.mult))
                e.drain()

        @block.gpsimd
        def _(e):
            for mt in range(64):
                e.wait_ge(vev.h, vev.b + 2 * mt + 2)
                SL = mk.SCR4[:, (mt % 2) * 2048:(mt % 2) * 2048 + 2048]
                wop[mt % 2].dinc(e.dma_start(
                    out=mk.ap("opart")[mt * 128:(mt + 1) * 128, :], in_=SL))
            e.wait_ge(wop[0].h, wop[0].b + 16 * 32)
            e.wait_ge(wop[1].h, wop[1].b + 16 * 32)
            scc.inc(e.collective_compute(
                "ReduceScatter", ALU.add, replica_groups=[list(range(NCORE))],
                ins=[mk.ap("opart")], outs=[mk.ap("ors")]))
        cc_rs = scc.n

        @block.sync
        def _(e):
            e.wait_ge(scc.h, cc_rs)
            for m in range(8):
                if m >= 2:
                    e.wait_ge(wxo2[m % 2].h, wxo2[m % 2].b + 16 * ((m - 2) // 2 + 1))
                fri[m % 2].dinc(e.dma_start(
                    out=mk.SCR4[:, (m % 2) * 2048:(m % 2) * 2048 + 2048],
                    in_=mk.ap("ors")[m * 128:(m + 1) * 128, :]))
                fxi[m % 2].dinc(e.dma_start(
                    out=mk.SCR4[:, 4096 + (m % 2) * 2048:4096 + (m % 2) * 2048 + 2048],
                    in_=mk.ap(xin)[m * 128:(m + 1) * 128, :]))

        @block.vector
        def _(e):
            for m in range(8):
                e.wait_ge(fri[m % 2].h, fri[m % 2].b + 16 * (m // 2 + 1))
                e.wait_ge(fxi[m % 2].h, fxi[m % 2].b + 16 * (m // 2 + 1))
                XO = mk.SCR4[:, 4096 + (m % 2) * 2048:4096 + (m % 2) * 2048 + 2048]
                vri.inc(e.tensor_tensor(
                    out=XO, in0=XO,
                    in1=mk.SCR4[:, (m % 2) * 2048:(m % 2) * 2048 + 2048],
                    op=ALU.add))
                e.drain()

        @block.gpsimd
        def _(e):
            for m in range(8):
                e.wait_ge(vri.h, vri.b + m + 1)
                XO = mk.SCR4[:, 4096 + (m % 2) * 2048:4096 + (m % 2) * 2048 + 2048]
                wxo2[m % 2].dinc(e.dma_start(
                    out=mk.ap(xout)[m * 128:(m + 1) * 128, :], in_=XO))

    def ph_fini(self):
        mk, block = self, self.block
        sfin = self.sem("sfin")
        scc = self.sem("scc")

        @block.gpsimd
        def _(e):
            for s in self.sems.values():
                if s.n > 0 and s is not sfin:
                    e.wait_ge(s.h, s.n)
            if "prep" in self.dbg:
                sfin.dinc(e.dma_start(out=mk.ap("dbg_wsc"), in_=mk.WSC[:]))
                sfin.dinc(e.dma_start(out=mk.ap("dbg_vec"), in_=mk.VEC[:, 0:192]))
                for w in ("gate", "qkv"):
                    sfin.dinc(e.dma_start(out=mk.ap("dbg_tw_" + w), in_=mk.ap("tw_" + w)))
            sfin.dinc(e.dma_start(out=mk.ap("y_out"), in_=mk.ap(self.xfin)))
            e.wait_ge(sfin.h, sfin.n)


def shard_inputs(inputs):
    x = np.ascontiguousarray(np.asarray(inputs["L0_attn_hidden_states"],
                                        dtype=np.float32).reshape(8192, H))
    consts = host_constants()
    namemap = {
        "qkv": "L0_attn_w_qkv", "out0": "L0_attn_w_out", "gate": "L0_ffn_w_gate",
        "up": "L0_ffn_w_up", "down": "L0_ffn_w_down", "inp": "L1_attn_in_proj_w",
        "outp": "L1_attn_out_proj_w", "l1up": "L1_ffn_w_up", "l1dn": "L1_ffn_w_down",
    }
    f32 = lambda k: np.asarray(inputs[k], dtype=np.float32)
    b_qkv = f32("L0_attn_b_qkv")
    xpw = f32("L1_attn_x_proj_w")
    dtw = f32("L1_attn_dt_proj_w")
    dtb = f32("L1_attn_dt_proj_b")
    cw = f32("L1_attn_conv_w")
    cb = f32("L1_attn_conv_b")
    alog = f32("L1_attn_A_log")
    dv = f32("L1_attn_D")
    in_maps = []
    for r in range(NCORE):
        m = {"x_in": x[r * TC:(r + 1) * TC]}
        for w, (rr, cc) in BIGW.items():
            full = np.ascontiguousarray(np.asarray(inputs[namemap[w]], dtype=np.float32))
            rs = rr // NCORE
            if w == "inp":
                m["wsl_" + w] = np.ascontiguousarray(np.concatenate(
                    [full[DS * r:DS * (r + 1)],
                     full[DI + DS * r:DI + DS * (r + 1)]], axis=0))
            elif w == "outp":
                m["wsl_" + w] = np.ascontiguousarray(full.T[r * rs:(r + 1) * rs])
            else:
                m["wsl_" + w] = full[r * rs:(r + 1) * rs]
        sl = slice(DS * r, DS * (r + 1))
        m["b_qkvT"] = np.ascontiguousarray(b_qkv.reshape(48, 128).T)
        m["b_out"] = np.ascontiguousarray(f32("L0_attn_b_out"))
        m["xpw_slT"] = np.ascontiguousarray(xpw[:, sl].T)
        m["dtw_slT"] = np.ascontiguousarray(dtw[sl, :].T)
        m["dtb_slT"] = np.ascontiguousarray(dtb[sl].reshape(4, 128).T)
        m["dv_slT"] = np.ascontiguousarray(dv[sl].reshape(4, 128).T)
        m["cw_slT"] = np.ascontiguousarray(
            cw[sl].reshape(4, 128, 4).transpose(1, 0, 2).reshape(128, 16))
        m["cb_slT"] = np.ascontiguousarray(cb[sl].reshape(4, 128).T)
        m["alog_slE"] = np.ascontiguousarray(np.tile(alog[sl, :], (1, 128)))
        for nm, v in consts.items():
            m[nm] = v
        in_maps.append(m)
    return in_maps


def kernel(**inputs):
    mk = MK()
    nc = mk.build()
    in_maps = shard_inputs(inputs)
    res = run_bass_kernel_spmd(nc, in_maps, core_ids=list(range(NCORE)))
    out = np.concatenate([res.results[r]["y_out"] for r in range(NCORE)], axis=0)
    return out.reshape(1, 8192, H).astype(np.float32)
